# version 17
# baseline (speedup 1.0000x reference)
import numpy as np
import ml_dtypes
import concourse.bass as bass
import concourse.mybir as mybir
from concourse.bass_utils import run_bass_kernel_spmd
from contextlib import ExitStack

BF = mybir.dt.bfloat16
F32 = mybir.dt.float32
AF = mybir.ActivationFunctionType
AL = mybir.AluOpType
AX = mybir.AxisListType.X

D = 4096
T_ALL = 4096
T_OWN = 2048
FH = 2048
NHL = 8
SCALE = 1.0 / 16.0
EPS = 1e-5
T_PASS = 1024
TP3 = 512
N_PASS1 = T_ALL // T_PASS
N_PASS3 = T_OWN // TP3

DEBUG = False



def _build_tables():
    g = [1.0, 1.0, 1.0]
    dim, n = 3, 8
    masks = sorted(range(n), key=lambda m: (bin(m).count("1"), m))
    inv = {m: i for i, m in enumerate(masks)}
    C = np.zeros((n, n, n), dtype=np.float32)
    for i, a in enumerate(masks):
        for j, b in enumerate(masks):
            sw, at = (0, a >> 1)
            while at:
                sw += bin(at & b).count("1")
                at >>= 1
            s = -1.0 if sw & 1 else 1.0
            for t in range(dim):
                if a >> t & 1 and b >> t & 1:
                    s *= float(g[t])
            C[i, j, inv[a ^ b]] = s
    grades = np.array([bin(m).count("1") for m in masks])
    rev = np.where(grades * (grades - 1) // 2 % 2 == 1, -1.0, 1.0).astype(np.float32)
    return C, rev


_C_TAB, _REV = _build_tables()
_BIV3 = np.ascontiguousarray(
    _C_TAB[:, :, [4, 5, 6]].transpose(2, 0, 1).reshape(3, 64)
).astype(np.float32)
_REVI = np.repeat(_REV, 8).reshape(64, 1).astype(np.float32)
_DIAGI = np.zeros((64, 1), np.float32)
for _i in range(8):
    _DIAGI[_i * 8 + _i, 0] = _REV[_i] * SCALE

ENGS = ("PE", "DVE", "ACT", "SP", "PL")


class SemC:
    __slots__ = ("name", "count", "h")

    def __init__(self, name):
        self.name = name
        self.count = 0
        self.h = None

    def tick(self, n=1):
        self.count += n
        return self.count


class Plan:
    def __init__(self):
        self.q = {e: [] for e in ENGS}
        self.sems = []
        self.self_sem = {"DVE": self.sem("dve_self"), "ACT": self.sem("act_self")}
        self.last_evt = {"DVE": None, "ACT": None}

    def sem(self, name):
        s = SemC(name)
        self.sems.append(s)
        return s

    def op(self, eng, fn, wait=(), inc=None, selfser=True):
        ss = self.self_sem.get(eng) if selfser else None
        prev_evt = self.last_evt.get(eng) if ss is not None else None
        my_inc = inc
        if ss is not None:
            if my_inc is None:
                my_inc = (ss, 1)
                ss.tick()
            self.last_evt[eng] = (my_inc[0], my_inc[0].count)
        wait = tuple((sb, v) for sb, v in wait if v > 0)

        def run(e):
            for sb, v in wait:
                e.wait_ge(sb.h, v)
            if prev_evt is not None:
                e.wait_ge(prev_evt[0].h, prev_evt[1])
            ins = fn(e)
            if my_inc is not None:
                sb, n = my_inc
                ins.then_inc(sb.h, n)

        self.q[eng].append(run)

    def dma(self, eng, fn, wait=(), inc=None):
        self.op(eng, fn, wait=wait, inc=inc, selfser=False)

    def all_done(self, eng):
        ss = self.self_sem[eng]
        w = [(ss, ss.count)]
        ev = self.last_evt.get(eng)
        if ev is not None:
            w.append(ev)
        return w

    def barrier_waits(self):
        return [(s, s.count) for s in self.sems if s.count > 0]

    def barrier(self):
        w = self.barrier_waits()
        for eng in ENGS:
            self.op(eng, lambda e: e.nop(), wait=w, selfser=False)


def build_program():
    nc = bass.Bass("TRN2", target_bir_lowering=False)
    P = Plan()
    es0 = ExitStack()

    def din(name, shape, dt=F32):
        return nc.dram_tensor(name, list(shape), dt, kind="ExternalInput").ap()

    xb = din("xb", (T_ALL, D))
    wqT = din("wqT", (D, FH))
    wkT = din("wkT", (D, FH))
    wvT = din("wvT", (D, FH))
    wbT = din("wbT", (D, D))
    wrT = din("wrT", (D, 8))
    aT = din("aT", (D, 128))
    bmT = din("bmT", (128, D))
    fbias = din("fbias", (1, D))
    bq_c = din("bq_c", (128, 16))
    bk_c = din("bk_c", (128, 16))
    bv_r = din("bv_r", (1, FH))
    bwT3 = din("bwT3", (3, 8))
    biv3 = din("biv3", (3, 64))
    revI = din("revI", (64, 1))
    diagI = din("diagI", (64, 1))
    idbf = din("idbf", (128, 128), BF)
    idf32 = din("idf32", (128, 128))
    ones_r_bf = din("ones_r_bf", (1, 128), BF)
    ones_c_bf = din("ones_c_bf", (128, 1), BF)
    rrep = din("rrep", (8, 128))
    sel0in = din("sel0", (128, 1))
    sel1in = din("sel1", (128, 1))

    out = nc.dram_tensor("out", [T_OWN, D], F32, kind="ExternalOutput").ap()

    Qfm = nc.dram_tensor("Qfm", [FH, T_ALL], BF, kind="Internal").ap()
    Kfm = nc.dram_tensor("Kfm", [FH, T_ALL], BF, kind="Internal").ap()
    Vtok = nc.dram_tensor("Vtok", [T_ALL, FH], BF, kind="Internal").ap()
    attn_own = nc.dram_tensor("attn_own", [T_OWN, FH], F32, kind="Internal").ap()
    contribs = [nc.dram_tensor(f"contrib{k}", [T_OWN, 512], BF, kind="Internal").ap()
                for k in range(4)]
    gaths = [nc.dram_tensor(f"gath{k}", [2 * T_OWN, 512], BF, kind="Internal").ap()
             for k in range(4)]
    ntab = nc.dram_tensor("ntab", [8, 64], BF, kind="Internal").ap()
    x1d = nc.dram_tensor("x1d", [T_OWN, D], F32, kind="Internal").ap()

    dbg = {}
    if DEBUG:
        for nm, shp in (("q", (FH, T_ALL)), ("k", (FH, T_ALL))):
            dbg[nm] = nc.dram_tensor("dbg_" + nm, list(shp), BF,
                                     kind="ExternalOutput").ap()
        dbg["v"] = nc.dram_tensor("dbg_v", [T_ALL, FH], BF,
                                  kind="ExternalOutput").ap()
        dbg["attn"] = nc.dram_tensor("dbg_attn", [T_OWN, FH], F32,
                                     kind="ExternalOutput").ap()
        dbg["c"] = nc.dram_tensor("dbg_c", [NHL, 24], F32,
                                  kind="ExternalOutput").ap()
        dbg["pt"] = nc.dram_tensor("dbg_pt", [128, 1024], BF,
                                   kind="ExternalOutput").ap()
        dbg["kb"] = nc.dram_tensor("dbg_kb", [128, 2 * T_ALL], BF,
                                   kind="ExternalOutput").ap()
        dbg["lg"] = nc.dram_tensor("dbg_lg", [128, 1024], F32,
                                   kind="ExternalOutput").ap()
        dbg["qb"] = nc.dram_tensor("dbg_qb", [128, 2 * T_ALL], BF,
                                   kind="ExternalOutput").ap()
        dbg["vb"] = nc.dram_tensor("dbg_vb", [128, 32 * 257], BF,
                                   kind="ExternalOutput").ap()
        dbg["x1"] = nc.dram_tensor("dbg_x1", [T_OWN, D], F32,
                                   kind="ExternalOutput").ap()

    def sb(stack, name, shape, dt):
        return stack.enter_context(nc.sbuf_tensor(name, list(shape), dt))

    c_biv = sb(es0, "c_biv", (3, 64), F32)
    c_bw = sb(es0, "c_bw", (3, 8), F32)
    c_revI = sb(es0, "c_revI", (64, 1), F32)
    c_diagI = sb(es0, "c_diagI", (64, 1), F32)
    c_idbf = sb(es0, "c_idbf", (128, 128), BF)
    c_idf32 = sb(es0, "c_idf32", (128, 128), F32)
    c_ones_r = sb(es0, "c_ones_r", (1, 128), BF)
    c_ones_c = sb(es0, "c_ones_c", (128, 1), BF)
    c_rrep = sb(es0, "c_rrep", (8, 128), F32)
    c_sel0 = sb(es0, "c_sel0", (128, 1), F32)
    c_sel1 = sb(es0, "c_sel1", (128, 1), F32)
    c_bq = sb(es0, "c_bq", (128, 16), F32)
    c_bk = sb(es0, "c_bk", (128, 16), F32)
    c_bv_bf = sb(es0, "c_bv_bf", (1, FH), BF)
    c_fb_bf = sb(es0, "c_fb_bf", (1, D), BF)
    s_nt = sb(es0, "s_nt", (64, 8), F32)
    s_nt2 = sb(es0, "s_nt2", (64, 8), F32)
    s_nth = sb(es0, "s_nth", (8, 64), BF)
    def mk_ln_tiles(stack, tag, nx=2):
        xnt2 = [sb(stack, f"xnt{tag}{i}", (128, D), BF) for i in range(nx)]
        lns2 = [{k: sb(stack, f"ln{tag}{i}_" + k, (128, 1), F32) for k in
                 ("s1", "s2", "m", "b2", "mm", "var", "veps", "sq", "rstd",
                  "negmr")} for i in range(2)]
        return xnt2, lns2

    psA = es0.enter_context(nc.psum_tensor("psA", [128, 1024], F32))
    psB = es0.enter_context(nc.psum_tensor("psB", [128, 1024], F32))
    psC = es0.enter_context(nc.psum_tensor("psC", [128, 1024], F32))
    psD = es0.enter_context(nc.psum_tensor("psD", [128, 512], F32))
    psE = es0.enter_context(nc.psum_tensor("psE", [128, 512], F32))

    s_x = P.sem("s_x")
    s_w = P.sem("s_w")
    s_pe = P.sem("s_pe")
    s_dve = P.sem("s_dve")
    s_act = P.sem("s_act")
    s_sp = P.sem("s_sp")
    s_ld2 = P.sem("s_ld2")
    s_st = P.sem("s_st")
    s_x1 = P.sem("s_x1")
    s_xp = [P.sem("s_xp0"), P.sem("s_xp1")]
    s_wp = [P.sem("s_wp0"), P.sem("s_wp1")]
    s_misc = P.sem("s_misc")
    s_cc = [P.sem(f"s_cc{k}") for k in range(4)]

    ps_free = {"D": [], "E": []}

    for dst, src in ((c_biv, biv3), (c_bw, bwT3), (c_revI, revI),
                     (c_diagI, diagI), (c_idbf, idbf), (c_idf32, idf32),
                     (c_ones_r, ones_r_bf), (c_ones_c, ones_c_bf),
                     (c_rrep, rrep), (c_sel0, sel0in), (c_sel1, sel1in),
                     (c_bq, bq_c), (c_bk, bk_c)):
        s_x.tick(16)
        P.dma("SP", lambda e, d=dst, s=src: e.dma_start(d[:, :], s),
              inc=(s_x, 16))
    s_misc.tick(16)
    P.dma("PL", lambda e: e.dma_start(c_bv_bf[:, :], bv_r), inc=(s_misc, 16))
    s_misc.tick(16)
    P.dma("PL", lambda e: e.dma_start(c_fb_bf[:, :], fbias), inc=(s_misc, 16))
    rows_tok = s_misc.count
    ld_all = s_x.count

    pe1 = s_pe.tick()
    P.op("PE", lambda e: e.matmul(psD[:64, 0:8], c_biv[:, :], c_bw[:, :],
                                  start=True, stop=True),
         wait=[(s_x, ld_all)], inc=(s_pe, 1), selfser=False)
    d_nt0 = s_dve.tick()
    P.op("DVE", lambda e: e.tensor_scalar_mul(s_nt[:, :], psD[:64, 0:8],
                                              c_revI[:, 0:1]),
         wait=[(s_pe, pe1)], inc=(s_dve, 1))
    ps_free["D"] = [(s_dve, d_nt0)]
    d1 = s_dve.tick()
    P.op("DVE", lambda e: e.tensor_scalar_add(s_nt2[:, :], s_nt[:, :],
                                              c_diagI[:, 0:1]), inc=(s_dve, 1))
    pe2 = s_pe.tick()
    P.op("PE", lambda e: e.matmul(psE[:8, 0:64], s_nt2[:, :], c_idf32[:64, :64],
                                  start=True, stop=True),
         wait=[(s_dve, d1)], inc=(s_pe, 1), selfser=False)
    d2 = s_dve.tick()
    P.op("DVE", lambda e: e.tensor_copy(s_nth[:, :], psE[:8, 0:64]),
         wait=[(s_pe, pe2)], inc=(s_dve, 1))
    ps_free["E"] = [(s_dve, d2)]
    t_nt = s_sp.tick(16)
    P.dma("PL", lambda e: e.dma_start(ntab, s_nth[:, :]),
          wait=[(s_dve, d2)], inc=(s_sp, 16))

    es1 = ExitStack()
    xnT = sb(es1, "xnT", (128, 32 * T_PASS), BF)
    W1 = 256
    wbuf = [sb(es1, f"wbuf{i}", (128, 32 * W1), BF) for i in range(2)]
    xt = [sb(es1, f"xt{i}", (128, D), F32) for i in range(2)]
    qraw = sb(es1, "qraw", (128, T_PASS), BF)
    evb = [sb(es1, f"evb{i}", (128, T_PASS), BF) for i in range(2)]
    vevb = [sb(es1, f"vevb{i}", (128, 512), BF) for i in range(2)]
    btiles = sb(es1, "btiles", (128, NHL * 128), BF)
    xnt2, lns2 = mk_ln_tiles(es1, "p1", nx=4)
    pdump = sb(es1, "pdump1", (128, D), BF)

    dz = s_dve.tick()
    P.op("DVE", lambda e: e.memset(btiles[:, :], 0.0), inc=(s_dve, 1))
    for h in range(NHL):
        for db in range(16):
            s_sp.tick(16)
            P.dma("PL",
                  lambda e, h=h, db=db: e.dma_start(
                      btiles[db * 8:db * 8 + 8,
                             h * 128 + db * 8: h * 128 + db * 8 + 8],
                      ntab[h:h + 1, :].rearrange("1 (i j) -> i j", i=8)),
                  wait=[(s_sp, t_nt), (s_dve, dz)], inc=(s_sp, 16))
    btiles_tok = s_sp.count

    def ln_sums(xtile, ready, lns, dump):
        a1 = s_act.tick()
        P.op("ACT", lambda e: e.activation(dump[:, :], xtile[:, :],
                                           AF.Identity,
                                           accum_out=lns["s1"][:, :]),
             wait=ready, inc=(s_act, 1))
        a2 = s_act.tick()
        P.op("ACT", lambda e: e.activation(dump[:, :], xtile[:, :], AF.Square,
                                           accum_out=lns["s2"][:, :]),
             inc=(s_act, 1))
        return a2

    def ln_finish(xtile, a2, lns, dst_bf):
        P.op("DVE", lambda e: e.tensor_single_scalar(lns["m"][:, :],
                                                     lns["s1"][:, :],
                                                     1.0 / D, AL.mult),
             wait=[(s_act, a2)])
        st_tok = s_dve.tick()
        P.op("DVE", lambda e: e.tensor_single_scalar(lns["b2"][:, :],
                                                     lns["s2"][:, :],
                                                     1.0 / D, AL.mult),
             inc=(s_dve, 1))
        P.op("DVE", lambda e: e.tensor_mul(lns["mm"][:, :], lns["m"][:, :],
                                           lns["m"][:, :]))
        P.op("DVE", lambda e: e.tensor_sub(lns["var"][:, :], lns["b2"][:, :],
                                           lns["mm"][:, :]))
        dv = s_dve.tick()
        P.op("DVE", lambda e: e.tensor_single_scalar(lns["veps"][:, :],
                                                     lns["var"][:, :], EPS,
                                                     AL.add), inc=(s_dve, 1))
        a3 = s_act.tick()
        P.op("ACT", lambda e: e.activation(lns["sq"][:, :], lns["veps"][:, :],
                                           AF.Sqrt),
             wait=[(s_dve, dv)], inc=(s_act, 1))
        P.op("DVE", lambda e: e.reciprocal(lns["rstd"][:, :], lns["sq"][:, :]),
             wait=[(s_act, a3)])
        P.op("DVE", lambda e: e.tensor_scalar(lns["negmr"][:, :], lns["m"][:, :],
                                              lns["rstd"][:, 0:1], -1.0,
                                              AL.mult, AL.mult))
        dn = s_dve.tick()
        P.op("DVE", lambda e: e.tensor_scalar(dst_bf[:, :], xtile[:, :],
                                              lns["rstd"][:, 0:1],
                                              lns["negmr"][:, 0:1],
                                              AL.mult, AL.add), inc=(s_dve, 1))
        return dn, st_tok

    def emit_transposes(src_bf, src_tok, toff, xnT_t):
        last_a = 0
        last_d = 0
        for grp in range(8):
            bank = "D" if grp % 2 == 0 else "E"
            pst = psD if grp % 2 == 0 else psE
            pe_t = s_pe.tick()
            for k in range(4):
                dc = grp * 4 + k

                def mmt(e, dc=dc, k=k, pst=pst):
                    return e.matmul(pst[:, k * 128:(k + 1) * 128],
                                    src_bf[:, dc * 128:(dc + 1) * 128],
                                    c_idbf[:, :], start=True, stop=True)
                w = ()
                if k == 0:
                    w = ps_free[bank] + [(s_dve, src_tok)]
                P.op("PE", mmt, wait=w,
                     inc=(s_pe, 1) if k == 3 else None, selfser=False)

            def evac(e, grp=grp, pst=pst, toff=toff, xnT_t=xnT_t):
                dst = xnT_t[:, :].rearrange("p (dc q) -> p dc q", dc=32)
                d = dst[:, grp * 4:(grp + 1) * 4, toff:toff + 128]
                if grp % 2 == 1:
                    return e.copy(d, pst[:, 0:512])
                return e.tensor_copy(d, pst[:, 0:512])
            if grp % 2 == 1:
                last_a = s_act.tick()
                P.op("ACT", evac, wait=[(s_pe, pe_t)], inc=(s_act, 1))
                ps_free[bank] = [(s_act, last_a)]
            else:
                last_d = s_dve.tick()
                P.op("DVE", evac, wait=[(s_pe, pe_t)], inc=(s_dve, 1))
                ps_free[bank] = [(s_dve, last_d)]
        return [(s_act, last_a), (s_dve, last_d)]

    def emit_wload(src_ap, col0, buf, wbuf_t, extra_wait=(), width=512):
        sem = s_wp[buf % 2]
        for part in range(4):
            sem.tick(16)

            def ld(e, part=part, col0=col0, buf=buf, src_ap=src_ap,
                   wbuf_t=wbuf_t, width=width):
                src = src_ap.rearrange("(dc p) f -> p dc f", p=128)
                dst = wbuf_t[buf][:, :].rearrange("p (dc f) -> p dc f",
                                                  f=width)
                return e.dma_start(dst[:, part * 8:(part + 1) * 8, :],
                                   src[:, part * 8:(part + 1) * 8,
                                       col0:col0 + width])
            P.dma("PL", ld, wait=extra_wait if part == 0 else (),
                  inc=(sem, 16))
        return (sem, sem.count)

    x_free = [0, 0]
    ln_free = [0, 0]
    wq_tok = emit_wload(wqT, 0, 0, wbuf, width=W1)
    pe_buf_use = {0: 0, 1: 0}
    tr_tok = []
    ln_sums_tok = {}
    ln_dn = {}

    def p1_load_sums(p, i):
        t0 = p * T_PASS
        bi = i % 2
        tokx = s_xp[bi].tick(16)
        P.dma("SP", lambda e, t0=t0, i=i, bi=bi: e.dma_start(
            xt[bi][:, :], xb[t0 + i * 128: t0 + (i + 1) * 128, :]),
            wait=[(s_dve, x_free[bi])], inc=(s_xp[bi], 16))
        ln_sums_tok[i] = ln_sums(xt[bi],
                                 [(s_xp[bi], tokx), (s_dve, ln_free[bi])],
                                 lns2[bi], pdump)

    def p1_finish(p, i):
        bi = i % 2
        dn, st_t = ln_finish(xt[bi], ln_sums_tok[i], lns2[bi], xnt2[i % 4])
        x_free[bi] = dn
        ln_free[bi] = st_t
        ln_dn[i] = dn

    def p1_transp(p, i):
        return emit_transposes(xnt2[i % 4], ln_dn[i], i * 128, xnT)

    def next_pass_drains(np_):
        return {0: [lambda: p1_load_sums(np_, 0)],
                1: [lambda: p1_load_sums(np_, 1)],
                2: [lambda: p1_finish(np_, 0)],
                3: [lambda: p1_load_sums(np_, 2)],
                4: [lambda: p1_finish(np_, 1)],
                5: [lambda: p1_load_sums(np_, 3)],
                6: [lambda: p1_finish(np_, 2)],
                7: [lambda: p1_finish(np_, 3)],
                8: [lambda: p1_load_sums(np_, 4)],
                9: [lambda: p1_load_sums(np_, 5)]}

    def next_pass_tail(np_):
        global_tr = None
        for i in range(4):
            global_tr = p1_transp(np_, i)
        p1_finish(np_, 4)
        global_tr = p1_transp(np_, 4)
        p1_finish(np_, 5)
        global_tr = p1_transp(np_, 5)
        p1_load_sums(np_, 6)
        p1_finish(np_, 6)
        global_tr = p1_transp(np_, 6)
        p1_load_sums(np_, 7)
        p1_finish(np_, 7)
        global_tr = p1_transp(np_, 7)
        return global_tr

    for i in range(2):
        p1_load_sums(0, i)
    p1_finish(0, 0)
    p1_load_sums(0, 2)
    p1_finish(0, 1)
    p1_load_sums(0, 3)
    p1_finish(0, 2)
    p1_finish(0, 3)
    p1_load_sums(0, 4)
    p1_load_sums(0, 5)
    tr_tok = next_pass_tail(0)

    for p in range(N_PASS1):
        t0 = p * T_PASS
        drains = next_pass_drains(p + 1) if p + 1 < N_PASS1 else {}
        nblk = FH // W1
        blocks = ([("Q", wqT, k) for k in range(nblk)] +
                  [("K", wkT, k) for k in range(nblk)] +
                  [("V", wvT, k) for k in range(nblk)])
        wtoks = {0: wq_tok}
        ev_toks = {}
        for bidx, (mat, wap, kblk) in enumerate(blocks):
            buf = bidx % 2
            for fn_ in drains.get(bidx, ()):
                fn_()
            if bidx + 1 < len(blocks):
                _, nap, nk = blocks[bidx + 1]
                wtoks[bidx + 1] = emit_wload(
                    nap, nk * W1, (bidx + 1) % 2, wbuf,
                    extra_wait=[(s_pe, pe_buf_use[(bidx + 1) % 2])],
                    width=W1)
            if mat in ("Q", "K"):
                for fc4 in range(W1 // 128):
                    fc = kblk * (W1 // 128) + fc4
                    pacc = psA if fc % 2 == 0 else psB
                    pe_tok = s_pe.tick()
                    for dc in range(32):
                        for g2 in range(2):

                            def mm(e, dc=dc, g2=g2, fc4=fc4, buf=buf, pacc=pacc):
                                ws = wbuf[buf][:, :].rearrange(
                                    "p (dc f) -> p dc f", f=W1)
                                return e.matmul(
                                    pacc[:, g2 * 512:(g2 + 1) * 512],
                                    ws[:, dc, fc4 * 128:(fc4 + 1) * 128],
                                    xnT[:, dc * T_PASS + g2 * 512:
                                        dc * T_PASS + (g2 + 1) * 512],
                                    start=(dc == 0), stop=(dc == 31))
                            w = ()
                            if dc == 0 and g2 == 0:
                                w = ([wtoks[bidx],
                                      (s_act, ev_toks.get(("qk", fc % 2), 0)),
                                      (s_dve, ev_toks.get(("fold", fc % 2), 0))]
                                     + tr_tok)
                            P.op("PE", mm, wait=w,
                                 inc=(s_pe, 1) if (dc == 31 and g2 == 1) else None,
                                 selfser=False)
                    pe_buf_use[buf] = pe_tok
                    bias_col = c_bq if mat == "Q" else c_bk
                    dst = qraw if mat == "Q" else evb[fc % 2]
                    a_ev = s_act.tick()
                    P.op("ACT", lambda e, pacc=pacc, dst=dst, bias_col=bias_col,
                         fc=fc: e.add(dst[:, :], pacc[:, :],
                                      bias_col[:, fc:fc + 1]),
                         wait=[(s_pe, pe_tok), (s_sp, s_sp.count)],
                         inc=(s_act, 1))
                    ev_toks[("qk", fc % 2)] = a_ev
                    if mat == "Q":
                        hloc = fc // 2
                        pe_f = s_pe.tick()
                        for g2 in range(2):

                            def mf(e, g2=g2, hloc=hloc):
                                return e.matmul(
                                    psC[:, g2 * 512:(g2 + 1) * 512],
                                    btiles[:, hloc * 128:(hloc + 1) * 128],
                                    qraw[:, g2 * 512:(g2 + 1) * 512],
                                    start=True, stop=True)
                            w = ()
                            if g2 == 0:
                                w = [(s_act, a_ev), (s_sp, btiles_tok),
                                     (s_dve, ev_toks.get("foldC", 0))]
                            P.op("PE", mf, wait=w,
                                 inc=(s_pe, 1) if g2 == 1 else None,
                                 selfser=False)
                        d_f = s_dve.tick()
                        P.op("DVE", lambda e, fc=fc: e.tensor_copy(
                            evb[fc % 2][:, :], psC[:, :]),
                            wait=[(s_pe, pe_f), (s_sp, s_sp.count)],
                            inc=(s_dve, 1))
                        ev_toks["foldC"] = d_f
                        ev_toks[("fold", fc % 2)] = d_f
                        src_sp = d_f
                        dstap = Qfm
                    else:
                        src_sp = None
                        dstap = Kfm
                    sp_tok = s_sp.tick(16)
                    P.dma("PL", lambda e, fc=fc, t0=t0, dstap=dstap:
                          e.dma_start(dstap[fc * 128:(fc + 1) * 128,
                                            t0:t0 + T_PASS], evb[fc % 2][:, :]),
                          wait=([(s_dve, src_sp)] if src_sp is not None
                                else [(s_act, a_ev)]),
                          inc=(s_sp, 16))
                    ev_toks[("spill", fc % 2)] = sp_tok
            else:
                for i in range(8):
                    pst = psD if i % 2 == 0 else psE
                    pe_tok = s_pe.tick()
                    for dc in range(32):

                        def mv(e, dc=dc, i=i, buf=buf, pst=pst):
                            ws = wbuf[buf][:, :].rearrange(
                                "p (dc f) -> p dc f", f=W1)
                            return e.matmul(
                                pst[:, 0:W1],
                                xnT[:, dc * T_PASS + i * 128:
                                    dc * T_PASS + (i + 1) * 128],
                                ws[:, dc, :], start=(dc == 0), stop=False)
                        w = ()
                        if dc == 0:
                            w = ([wtoks[bidx], (s_misc, rows_tok)]
                                 + ps_free["D" if i % 2 == 0 else "E"]
                                 + tr_tok)
                        P.op("PE", mv, wait=w, selfser=False)
                    P.op("PE", lambda e, kblk=kblk, pst=pst: e.matmul(
                        pst[:, 0:W1], c_ones_r[:, :],
                        c_bv_bf[0:1, kblk * W1:(kblk + 1) * W1],
                        start=False, stop=True),
                        inc=(s_pe, 1), selfser=False)
                    pe_buf_use[buf] = pe_tok
                    d_v = s_dve.tick()
                    P.op("DVE", lambda e, i=i, pst=pst: e.tensor_copy(
                        vevb[i % 2][:, 0:W1], pst[:, 0:W1]),
                        wait=[(s_pe, pe_tok), (s_sp, s_sp.count)],
                        inc=(s_dve, 1))
                    ev_toks[("v", i % 2)] = d_v
                    ps_free["D" if i % 2 == 0 else "E"] = [(s_dve, d_v)]
                    vsp_tok = s_sp.tick(16)
                    P.dma("PL", lambda e, i=i, kblk=kblk, t0=t0: e.dma_start(
                        Vtok[t0 + i * 128: t0 + (i + 1) * 128,
                             kblk * W1:(kblk + 1) * W1],
                        vevb[i % 2][:, 0:W1]),
                        wait=[(s_dve, d_v)], inc=(s_sp, 16))
                    ev_toks[("vspill", i % 2)] = vsp_tok
        if p + 1 < N_PASS1:
            wq_tok = emit_wload(wqT, 0, 0, wbuf,
                                extra_wait=[(s_pe, pe_buf_use[0])], width=W1)
            wtoks = None
            tr_tok = next_pass_tail(p + 1)
    if DEBUG:
        for (ap_s, ap_d, rows) in ((Qfm, dbg["q"], FH), (Kfm, dbg["k"], FH),
                                   (Vtok, dbg["v"], T_ALL)):
            for r0 in range(0, rows, 128):
                s_st.tick(16)
                P.dma("PL", lambda e, ap_s=ap_s, ap_d=ap_d, r0=r0: e.dma_start(
                    ap_d[r0:r0 + 128, :], ap_s[r0:r0 + 128, :]),
                    wait=[(s_sp, s_sp.count)], inc=(s_st, 16))

    es1.close()
    P.barrier()

    es2 = ExitStack()
    kbuf = [sb(es2, f"kbuf{i}", (128, 2 * T_ALL), BF) for i in range(2)]
    qbuf = [sb(es2, f"qbuf{i}", (128, 2 * T_ALL), BF) for i in range(2)]
    vbuf = [sb(es2, f"vbuf{i}", (128, 32 * 257), BF) for i in range(2)]
    probsT = sb(es2, "probsT", (128, 32 * 1024), BF)
    csq = sb(es2, "csq", (128, 2 * T_ALL), BF)
    att = [sb(es2, f"att{i}", (128, 256), F32) for i in range(2)]
    c2q = sb(es2, "c2q", (1, 16), F32)
    cC = {k: sb(es2, "cc_" + k, (1, 1), F32) for k in ("mq", "mk", "p", "ng")}
    cCb = sb(es2, "cc_ngb", (1, 1), BF)
    negCb = [sb(es2, f"negC{i}", (128, 1), F32) for i in range(2)]
    rc2 = sb(es2, "rc2", (128, 1), F32)
    dbg_lgs = sb(es2, "dbg_lgs", (128, 1024), F32) if DEBUG else None

    for b_ in range(2):
        s_dve.tick()
        P.op("DVE", lambda e, b_=b_: e.memset(
            vbuf[b_][:, :].rearrange("p (mc f) -> p mc f", f=257)[:, :, 256:257],
            1.0), inc=(s_dve, 1))
    vones_tok = s_dve.count

    def p2_loads(h, bi):
        guard = [(s_pe, s_pe.count)] + P.all_done("ACT")
        for fc in range(2):
            s_ld2.tick(16)
            P.dma("SP", lambda e, h=h, fc=fc, bi=bi: e.dma_start(
                kbuf[bi][:, fc * T_ALL:(fc + 1) * T_ALL],
                Kfm[h * 256 + fc * 128: h * 256 + fc * 128 + 128, :]),
                wait=guard if fc == 0 else (), inc=(s_ld2, 16))
            s_ld2.tick(16)
            P.dma("SP", lambda e, h=h, fc=fc, bi=bi: e.dma_start(
                qbuf[bi][:, fc * T_ALL:(fc + 1) * T_ALL],
                Qfm[h * 256 + fc * 128: h * 256 + fc * 128 + 128, :]),
                inc=(s_ld2, 16))
        for part in range(4):
            s_ld2.tick(16)

            def lv(e, h=h, part=part, bi=bi):
                vs = vbuf[bi][:, :].rearrange("p (mc f) -> p mc f", f=257)
                src = Vtok.rearrange("(mc p) f -> p mc f", p=128)
                return e.dma_start(vs[:, part * 8:(part + 1) * 8, 0:256],
                                   src[:, part * 8:(part + 1) * 8,
                                       h * 256:(h + 1) * 256])
            P.dma("SP", lv, wait=[(s_dve, vones_tok)] if part == 0 else (),
                  inc=(s_ld2, 16))
        return s_ld2.count

    ld_tok = {0: p2_loads(0, 0), 1: p2_loads(1, 1)}
    exp_free = {}
    av_free = {}
    att_free = {}

    def emit_cbound_squares(h, which):
        bi = h % 2
        bufs = qbuf if which == "q" else kbuf
        pe_before = s_pe.count
        a2 = None
        for fc in range(2):
            a2 = s_act.tick()
            P.op("ACT", lambda e, bufs=bufs, bi=bi, fc=fc: e.activation(
                csq[:, fc * T_ALL:(fc + 1) * T_ALL],
                bufs[bi][:, fc * T_ALL:(fc + 1) * T_ALL], AF.Square),
                wait=([(s_ld2, ld_tok[h]), (s_pe, pe_before)]
                      if fc == 0 else ()),
                inc=(s_act, 1))
        return a2

    def emit_cbound(h, sq_q=None):
        bi = h % 2
        negC = negCb[h % 2]
        for which, bufs in (("q", qbuf), ("k", kbuf)):
            base = 0 if which == "q" else 8
            if which == "q" and sq_q is not None:
                a2 = sq_q
            else:
                a2 = emit_cbound_squares(h, which)
            for lg in range(8):
                pst = psD if lg % 2 == 0 else psE
                pe_t = s_pe.tick()
                bank = "D" if lg % 2 == 0 else "E"
                for fc in range(2):

                    def msum(e, lg=lg, fc=fc, pst=pst):
                        return e.matmul(
                            pst[0:1, :], c_ones_c[:, :],
                            csq[:, fc * T_ALL + lg * 512:
                                fc * T_ALL + (lg + 1) * 512],
                            start=(fc == 0), stop=(fc == 1))
                    w = ([(s_act, a2)] + ps_free[bank]) if fc == 0 else ()
                    P.op("PE", msum, wait=w,
                         inc=(s_pe, 1) if fc == 1 else None, selfser=False)
                d_mx = s_dve.tick()
                P.op("DVE", lambda e, lg=lg, pst=pst, base=base: e.reduce_max(
                    c2q[:, base + lg:base + lg + 1], pst[0:1, :], axis=AX),
                    wait=[(s_pe, pe_t)], inc=(s_dve, 1))
                ps_free[bank] = [(s_dve, d_mx)]
        P.op("DVE", lambda e: e.reduce_max(cC["mq"][:, :], c2q[:, 0:8], axis=AX))
        P.op("DVE", lambda e: e.reduce_max(cC["mk"][:, :], c2q[:, 8:16], axis=AX))
        dp = s_dve.tick()
        P.op("DVE", lambda e: e.tensor_mul(cC["p"][:, :], cC["mq"][:, :],
                                           cC["mk"][:, :]), inc=(s_dve, 1))
        a_rt = s_act.tick()
        P.op("ACT", lambda e: e.activation(cC["ng"][:, :], cC["p"][:, :],
                                           AF.Sqrt),
             wait=[(s_dve, dp)], inc=(s_act, 1))
        d_ng = s_dve.tick()
        P.op("DVE", lambda e: e.tensor_single_scalar(cCb[:, :], cC["ng"][:, :],
                                                     -1.0, AL.mult),
             wait=[(s_act, a_rt)], inc=(s_dve, 1))
        pe_b = s_pe.tick()
        P.op("PE", lambda e, negC=negC: e.matmul(
            psD[:, 0:1], c_ones_r[:, :], cCb[:, :], start=True, stop=True),
             wait=[(s_dve, d_ng)] + ps_free["D"], inc=(s_pe, 1), selfser=False)
        d_nc = s_dve.tick()
        P.op("DVE", lambda e, negC=negC: e.tensor_copy(negC[:, :], psD[:, 0:1]),
             wait=[(s_pe, pe_b)] + P.all_done("ACT"), inc=(s_dve, 1))
        ps_free["D"] = [(s_dve, d_nc)]
        return d_nc

    d_nc_tok = {0: None}
    d_nc_tok[0] = emit_cbound(0)
    for h in range(NHL):
        bi = h % 2
        negC = negCb[h % 2]
        d_nc = d_nc_tok[h]

        if DEBUG and h == 0:
            for nm, buf in (("kb", kbuf[0]), ("qb", qbuf[0]), ("vb", vbuf[0])):
                s_st.tick(16)
                P.dma("PL", lambda e, nm=nm, buf=buf: e.dma_start(
                    dbg[nm][:, :], buf[:, :]),
                    wait=[(s_ld2, ld_tok[0])], inc=(s_st, 16))
        if DEBUG:
            s_st.tick(16)
            P.dma("PL", lambda e, h=h: e.dma_start(
                dbg["c"][h:h + 1, 0:16], c2q[:, :]),
                wait=[(s_dve, d_nc)], inc=(s_st, 16))
            s_st.tick(16)
            P.dma("PL", lambda e, h=h: e.dma_start(
                dbg["c"][h:h + 1, 16:17], negC[0:1, 0:1]),
                inc=(s_st, 16))

        for qi, lq in enumerate((2, 3, 0, 1)):
            if qi == 1 and h + 1 < NHL:
                sqq_next = emit_cbound_squares(h + 1, "q")
            if qi == 2 and h + 1 < NHL:
                d_nc_tok[h + 1] = emit_cbound(h + 1, sq_q=sqq_next)
            l0 = lq * 1024
            for mc in range(32):
                pacc = (psA, psB, psC)[mc % 3]
                pe_t = s_pe.tick()
                for fc in range(2):
                    for g2 in range(2):

                        def mqk(e, mc=mc, fc=fc, g2=g2, bi=bi, l0=l0,
                                pacc=pacc):
                            return e.matmul(
                                pacc[:, g2 * 512:(g2 + 1) * 512],
                                kbuf[bi][:, fc * T_ALL + mc * 128:
                                         fc * T_ALL + (mc + 1) * 128],
                                qbuf[bi][:, fc * T_ALL + l0 + g2 * 512:
                                         fc * T_ALL + l0 + (g2 + 1) * 512],
                                start=(fc == 0), stop=(fc == 1))
                        w = ()
                        if fc == 0 and g2 == 0:
                            w = [(s_dve, d_nc), (s_act, exp_free.get(mc % 3, 0))]
                            pass
                        P.op("PE", mqk, wait=w,
                             inc=(s_pe, 1) if (fc == 1 and g2 == 1) else None,
                             selfser=False)
                if False and DEBUG and h == 0 and lq == 0 and mc == 0:
                    d_lg = s_dve.tick()
                    P.op("DVE", lambda e, pacc=pacc: e.tensor_copy(
                        dbg_lgs[:, :], pacc[:, :]),
                        wait=[(s_pe, pe_t)], inc=(s_dve, 1))
                    s_st.tick(16)
                    P.dma("PL", lambda e: e.dma_start(dbg["lg"][:, :],
                                                      dbg_lgs[:, :]),
                          wait=[(s_dve, d_lg)], inc=(s_st, 16))
                a_e = s_act.tick()
                P.op("ACT", lambda e, mc=mc, pacc=pacc, negC=negC:
                     e.activation(
                         probsT[:, mc * 1024:(mc + 1) * 1024], pacc[:, :],
                         AF.Exp, bias=negC[:, 0:1], scale=1.0),
                    wait=[(s_pe, pe_t)], inc=(s_act, 1))
                exp_free[mc % 3] = a_e
            exp_all = s_act.count
            if DEBUG and h == 0 and lq == 0:
                s_st.tick(16)
                P.dma("PL", lambda e: e.dma_start(
                    dbg["pt"][:, :], probsT[:, 0:1024]),
                    wait=[(s_act, exp_all)], inc=(s_st, 16))
            for lc in range(8):
                bank = "D" if lc % 2 == 0 else "E"
                pav = psD if lc % 2 == 0 else psE
                pe_t = s_pe.tick()
                for mc in range(32):

                    def mav(e, mc=mc, lc=lc, bi=bi, pav=pav):
                        return e.matmul(
                            pav[:, 0:257],
                            probsT[:, mc * 1024 + lc * 128:
                                   mc * 1024 + lc * 128 + 128],
                            vbuf[bi][:, mc * 257:(mc + 1) * 257],
                            start=(mc == 0), stop=(mc == 31))
                    w = ()
                    if mc == 0:
                        w = [(s_act, exp_all)] + ps_free[bank]
                    P.op("PE", mav, wait=w,
                         inc=(s_pe, 1) if mc == 31 else None, selfser=False)
                P.op("DVE", lambda e, pav=pav: e.reciprocal(
                    rc2[:, :], pav[:, 256:257]), wait=[(s_pe, pe_t)])
                d_at = s_dve.tick()
                P.op("DVE", lambda e, pav=pav, lc=lc: e.tensor_scalar_mul(
                    att[lc % 2][:, :], pav[:, 0:256], rc2[:, 0:1]),
                    wait=[(s_sp, s_sp.count)], inc=(s_dve, 1))
                ps_free[bank] = [(s_dve, d_at)]
                lg_ = l0 + lc * 128
                sp_t = s_sp.tick(16)
                if lg_ < T_OWN:
                    P.dma("PL", lambda e, lg_=lg_, h=h, lc=lc: e.dma_start(
                        attn_own[lg_:lg_ + 128, h * 256:(h + 1) * 256],
                        att[lc % 2][:, :]),
                        wait=[(s_dve, d_at)], inc=(s_sp, 16))
                else:
                    k = h // 2
                    P.dma("PL", lambda e, lg_=lg_, h=h, lc=lc, k=k: e.dma_start(
                        contribs[k][lg_ - T_OWN:lg_ - T_OWN + 128,
                                    (h % 2) * 256:(h % 2) * 256 + 256],
                        att[lc % 2][:, :]),
                        wait=[(s_dve, d_at)], inc=(s_sp, 16))
                att_free[lc % 2] = sp_t
        if h % 2 == 1:
            k = h // 2
            s_cc[k].tick()
            P.dma("PL", lambda e, k=k: e.collective_compute(
                "AllGather", AL.bypass,
                replica_groups=[[0, 1], [2, 3], [4, 5], [6, 7]],
                ins=[contribs[k].opt()], outs=[gaths[k].opt()]),
                wait=[(s_sp, s_sp.count)], inc=(s_cc[k], 1))
        if h + 2 < NHL:
            ld_tok[h + 2] = p2_loads(h + 2, (h + 2) % 2)

    if DEBUG:
        for r0 in range(0, T_OWN, 128):
            s_st.tick(16)
            P.dma("PL", lambda e, r0=r0: e.dma_start(
                dbg["attn"][r0:r0 + 128, :], attn_own[r0:r0 + 128, :]),
                wait=[(s_sp, s_sp.count)], inc=(s_st, 16))

    es2.close()
    P.barrier()

    es3 = ExitStack()
    xnT3 = sb(es3, "xnT3", (128, 32 * TP3), BF)
    wbuf3 = [sb(es3, f"wbuf3{i}", (128, 32 * 512), BF) for i in range(2)]
    xt3 = sb(es3, "xt3", (128, D), F32)
    aot = sb(es3, "aot", (128, FH), F32)
    gcat = sb(es3, "gcat", (128, D), BF)
    x1full = sb(es3, "x1full", (128, D), F32)
    sibscr = aot
    abuf = sb(es3, "abuf", (128, 32 * 128), BF)
    bmb = sb(es3, "bmb", (128, D), BF)
    wrb = sb(es3, "wrb", (128, 32 * 8), BF)
    w8 = sb(es3, "w8", (128, 8), F32)
    xnt2, lns2 = mk_ln_tiles(es3, "p3")
    r8 = {}
    for k in ("m1", "m2", "nm", "ssum", "rc"):
        r8[k] = sb(es3, "r8_" + k, (128, 1), F32)
    for k in ("eq", "msk", "ke", "ee", "wts"):
        r8[k] = sb(es3, "r8_" + k, (128, 8), F32)
    wT_sb = sb(es3, "wT_sb", (8, TP3), F32)
    h_sb = sb(es3, "h_sb", (128, TP3), BF)
    g_sb = sb(es3, "g_sb", (128, TP3), BF)
    OBW = 512
    x1tt = [sb(es3, f"x1tt{i}", (128, OBW), F32) for i in range(2)]
    outt = [sb(es3, f"outt{i}", (128, OBW), F32) for i in range(2)]

    def psR():
        return psB[:, 0:8]

    def psW():
        return psC[0:8, 0:128]

    for part in range(2):
        s_misc.tick(16)
        P.dma("PL", lambda e, part=part: e.dma_start(
            abuf[:, :].rearrange("p (dc f) -> p dc f", f=128)
            [:, part * 16:(part + 1) * 16, :],
            aT.rearrange("(dc p) f -> p dc f", p=128)
            [:, part * 16:(part + 1) * 16, :]), inc=(s_misc, 16))
    s_misc.tick(16)
    P.dma("PL", lambda e: e.dma_start(bmb[:, :], bmT), inc=(s_misc, 16))
    s_misc.tick(16)
    P.dma("PL", lambda e: e.dma_start(
        wrb[:, :].rearrange("p (dc f) -> p dc f", f=8)[:, :, :],
        wrT.rearrange("(dc p) f -> p dc f", p=128)[:, :, :]), inc=(s_misc, 16))
    tok_a = tok_bm = tok_r = s_misc.count

    xa_free = 0
    sums3_tok = 0
    ln_free3 = [0, 0]
    x1_free = 0
    x1sp_free = 0
    tr_tok = []
    wT_tok = 0
    psr_free = []
    pe_buf3 = {0: 0, 1: 0}
    wt_next = None
    x1_rot = [0]
    x1_busy = [0, 0]
    x1sp_pass = {}
    for p in range(N_PASS3):
        t0 = p * TP3
        if wt_next is None:
            wt_next = emit_wload(wbT, 0, 0, wbuf3,
                                 extra_wait=[(s_pe, pe_buf3[0])], width=OBW)
        for i in range(TP3 // 128):
            tc = t0 + i * 128
            s_x.tick(16)
            P.dma("SP", lambda e, tc=tc: e.dma_start(
                xt3[:, :], xb[tc:tc + 128, :]),
                wait=[(s_dve, xa_free), (s_act, sums3_tok)], inc=(s_x, 16))
            s_x.tick(16)
            P.dma("SP", lambda e, tc=tc: e.dma_start(
                aot[:, :], attn_own[tc:tc + 128, :]), inc=(s_x, 16))
            for k in range(4):
                s_x.tick(16)
                P.dma("SP", lambda e, k=k, tc=tc: e.dma_start(
                    gcat[:, k * 512:(k + 1) * 512], gaths[k][tc:tc + 128, :]),
                    wait=[(s_cc[k], 1), (s_dve, xa_free)], inc=(s_x, 16))
                s_x.tick(16)
                P.dma("SP", lambda e, k=k, tc=tc: e.dma_start(
                    gcat[:, FH + k * 512: FH + (k + 1) * 512],
                    gaths[k][T_OWN + tc: T_OWN + tc + 128, :]),
                    inc=(s_x, 16))
            ldx = s_x.count
            P.op("DVE", lambda e: e.tensor_add(x1full[:, 0:FH], xt3[:, 0:FH],
                                               aot[:, :]),
                 wait=[(s_x, ldx), (s_dve, x1_free), (s_sp, x1sp_free)])
            P.op("DVE", lambda e: e.scalar_tensor_tensor(
                sibscr[:, :], gcat[:, 0:FH], c_sel0[:, 0:1], xt3[:, FH:D],
                AL.mult, AL.add))
            d_x1 = s_dve.tick()
            P.op("DVE", lambda e: e.scalar_tensor_tensor(
                x1full[:, FH:D], gcat[:, FH:D], c_sel1[:, 0:1], sibscr[:, :],
                AL.mult, AL.add), inc=(s_dve, 1))
            xa_free = d_x1
            x1sp_free = s_sp.tick(16)
            x1sp_pass[p] = x1sp_free
            P.dma("PL", lambda e, tc=tc: e.dma_start(
                x1d[tc:tc + 128, :], x1full[:, :]),
                wait=[(s_dve, d_x1)], inc=(s_sp, 16))
            a2s = ln_sums(x1full, [(s_dve, d_x1), (s_dve, ln_free3[i % 2])],
                          lns2[i % 2], xt3)
            sums3_tok = a2s
            dn, st_t = ln_finish(x1full, a2s, lns2[i % 2], xnt2[i % 2])
            x1_free = dn
            ln_free3[i % 2] = st_t
            tr_tok = emit_transposes(xnt2[i % 2], dn, i * 128, xnT3)
            pe_r = s_pe.tick()
            for dc in range(32):
                P.op("PE", lambda e, dc=dc, i=i: e.matmul(
                    psR(),
                    xnT3[:, dc * TP3 + i * 128: dc * TP3 + (i + 1) * 128],
                    wrb[:, :].rearrange("p (dc f) -> p dc f", f=8)[:, dc, :],
                    start=(dc == 0), stop=(dc == 31)),
                    wait=(([(s_misc, tok_r)] + tr_tok + psr_free)
                          if dc == 0 else ()),
                    inc=(s_pe, 1) if dc == 31 else None, selfser=False)
            P.op("DVE", lambda e: e.reduce_max(r8["m1"][:, :], psR(), axis=AX),
                 wait=[(s_pe, pe_r)])
            P.op("DVE", lambda e: e.tensor_single_scalar(
                r8["nm"][:, :], r8["m1"][:, :], -1.0, AL.mult))
            P.op("DVE", lambda e: e.tensor_single_scalar(
                r8["eq"][:, :], psR(), r8["m1"][:, 0:1], AL.is_ge))
            P.op("DVE", lambda e: e.scalar_tensor_tensor(
                r8["msk"][:, :], r8["eq"][:, :], -1e30, psR(),
                AL.mult, AL.add))
            P.op("DVE", lambda e: e.reduce_max(r8["m2"][:, :], r8["msk"][:, :],
                                               axis=AX))
            d_k = s_dve.tick()
            P.op("DVE", lambda e: e.tensor_single_scalar(
                r8["ke"][:, :], psR(), r8["m2"][:, 0:1], AL.is_ge),
                inc=(s_dve, 1))
            a_ee = s_act.tick()
            P.op("ACT", lambda e: e.activation(
                r8["ee"][:, :], psR(), AF.Exp, bias=r8["nm"][:, 0:1],
                scale=1.0), wait=[(s_dve, d_k)], inc=(s_act, 1))
            d_wts = s_dve.tick()
            P.op("DVE", lambda e: e.tensor_mul(
                r8["wts"][:, :], r8["ke"][:, :], r8["ee"][:, :]),
                wait=[(s_act, a_ee)], inc=(s_dve, 1))
            psr_free = [(s_act, a_ee), (s_dve, d_wts)]
            P.op("DVE", lambda e: e.reduce_sum(r8["ssum"][:, :], r8["wts"][:, :],
                                               axis=AX))
            P.op("DVE", lambda e: e.reciprocal(r8["rc"][:, :], r8["ssum"][:, :]))
            d_w = s_dve.tick()
            P.op("DVE", lambda e: e.tensor_scalar(
                w8[:, :], r8["wts"][:, :], r8["rc"][:, 0:1], 2.0,
                AL.mult, AL.mult), inc=(s_dve, 1))
            pe_w = s_pe.tick()
            P.op("PE", lambda e: e.matmul(psW(), w8[:, :], c_idf32[:, :],
                                          start=True, stop=True),
                 wait=[(s_dve, d_w), (s_dve, wT_tok)],
                 inc=(s_pe, 1), selfser=False)
            wT_tok = s_dve.tick()
            P.op("DVE", lambda e, i=i: e.tensor_copy(
                wT_sb[:, i * 128:(i + 1) * 128], psW()),
                wait=[(s_pe, pe_w)], inc=(s_dve, 1))

        pe_we = s_pe.tick()
        P.op("PE", lambda e: e.matmul(
            psC[:, 0:TP3], c_rrep[:, :], wT_sb[:, :], start=True, stop=True),
            wait=[(s_dve, wT_tok)], inc=(s_pe, 1), selfser=False)
        pe_h = s_pe.tick()
        for dc in range(32):
            P.op("PE", lambda e, dc=dc: e.matmul(
                psA[:, 0:TP3],
                abuf[:, dc * 128:(dc + 1) * 128],
                xnT3[:, dc * TP3:(dc + 1) * TP3],
                start=(dc == 0), stop=(dc == 31)),
                wait=([(s_misc, tok_a)] if dc == 0 else ()),
                inc=(s_pe, 1) if dc == 31 else None,
                selfser=False)
        P.op("DVE", lambda e: e.tensor_copy(h_sb[:, :], psA[:, 0:TP3]),
             wait=[(s_pe, pe_h)])
        d_g = s_dve.tick()
        P.op("DVE", lambda e: e.tensor_mul(g_sb[:, :], h_sb[:, :],
                                           psC[:, 0:TP3]),
             wait=[(s_pe, pe_we)], inc=(s_dve, 1))

        NOB = D // OBW
        wt_toks = {0: wt_next}
        wt_next = None
        ev_free = {}
        for ob in range(NOB):
            buf = ob % 2
            if ob + 1 < NOB:
                wt_toks[ob + 1] = emit_wload(
                    wbT, (ob + 1) * OBW, (ob + 1) % 2, wbuf3,
                    extra_wait=[(s_pe, pe_buf3[(ob + 1) % 2])], width=OBW)
            elif p + 1 < N_PASS3:
                wt_next = emit_wload(
                    wbT, 0, (ob + 1) % 2, wbuf3,
                    extra_wait=[(s_pe, pe_buf3[(ob + 1) % 2])], width=OBW)
            for i in range(TP3 // 128):
                slot = (ob * (TP3 // 128) + i) % 2
                pst = psD if slot == 0 else psE
                pe_t = s_pe.tick()
                for dc in range(32):

                    def mo(e, dc=dc, i=i, buf=buf, pst=pst):
                        ws = wbuf3[buf][:, :].rearrange(
                            "p (dc f) -> p dc f", f=OBW)
                        return e.matmul(
                            pst[:, 0:OBW],
                            xnT3[:, dc * TP3 + i * 128:
                                 dc * TP3 + (i + 1) * 128],
                            ws[:, dc, :], start=(dc == 0), stop=False)
                    w = ()
                    if dc == 0:
                        w = ([wt_toks[ob]]
                             + ps_free["D" if slot == 0 else "E"] + tr_tok)
                    P.op("PE", mo, wait=w, selfser=False)
                P.op("PE", lambda e, ob=ob, pst=pst: e.matmul(
                    pst[:, 0:OBW], c_ones_r[:, :],
                    c_fb_bf[0:1, ob * OBW:(ob + 1) * OBW],
                    start=False, stop=False), selfser=False)
                P.op("PE", lambda e, ob=ob, i=i, pst=pst: e.matmul(
                    pst[:, 0:OBW], g_sb[:, i * 128:(i + 1) * 128],
                    bmb[:, ob * OBW:(ob + 1) * OBW],
                    start=False, stop=True),
                    wait=[(s_dve, d_g), (s_misc, tok_bm)],
                    inc=(s_pe, 1), selfser=False)
                pe_buf3[buf] = pe_t
                tok1 = s_xp[slot].tick(16)
                P.dma("SP", lambda e, i=i, ob=ob, t0=t0, slot=slot:
                      e.dma_start(
                          x1tt[slot][:, :],
                          x1d[t0 + i * 128: t0 + (i + 1) * 128,
                              ob * OBW:(ob + 1) * OBW]),
                      wait=[(s_dve, x1_busy[slot]),
                            (s_sp, x1sp_pass[p])], inc=(s_xp[slot], 16))
                d_o = s_dve.tick()
                P.op("DVE", lambda e, pst=pst, slot=slot:
                     e.tensor_add(outt[slot][:, :], pst[:, 0:OBW],
                                  x1tt[slot][:, :]),
                     wait=[(s_pe, pe_t), (s_xp[slot], tok1),
                           (s_st, ev_free.get(("st", slot), 0))],
                     inc=(s_dve, 1))
                x1_busy[slot] = d_o
                st_t = s_st.tick(16)
                P.dma("SP", lambda e, i=i, ob=ob, t0=t0, slot=slot:
                      e.dma_start(
                          out[t0 + i * 128: t0 + (i + 1) * 128,
                              ob * OBW:(ob + 1) * OBW], outt[slot][:, :]),
                      wait=[(s_dve, d_o)], inc=(s_st, 16))
                ev_free[("st", slot)] = st_t
                ev_free[slot] = d_o
                ps_free["D" if slot == 0 else "E"] = [(s_dve, d_o)]

    if DEBUG:
        for r0 in range(0, T_OWN, 128):
            s_st.tick(16)
            P.dma("PL", lambda e, r0=r0: e.dma_start(
                dbg["x1"][r0:r0 + 128, :], x1d[r0:r0 + 128, :]),
                wait=[(s_sp, s_sp.count)], inc=(s_st, 16))

    P.op("SP", lambda e: e.nop(), wait=[(s_st, s_st.count)], selfser=False)

    es3.close()

    with ExitStack() as emit_es:
        for s in P.sems:
            s.h = emit_es.enter_context(nc.semaphore(s.name))
        with nc.Block() as blk:
            @blk.tensor
            def _(t):
                for fn in P.q["PE"]:
                    fn(t)

            @blk.vector
            def _(v):
                for fn in P.q["DVE"]:
                    fn(v)

            @blk.scalar
            def _(s):
                for fn in P.q["ACT"]:
                    fn(s)

            @blk.sync
            def _(s):
                for fn in P.q["SP"]:
                    fn(s)

            @blk.gpsimd
            def _(g):
                for fn in P.q["PL"]:
                    fn(g)
    es0.close()
    return nc


_NC_CACHE = None


def _get_nc():
    global _NC_CACHE
    if _NC_CACHE is None:
        _NC_CACHE = build_program()
    return _NC_CACHE


def make_in_maps(inputs):
    x = np.asarray(inputs["L0_attn_x"])
    Wq = np.asarray(inputs["L0_attn_Wq"])
    bq = np.asarray(inputs["L0_attn_bq"])
    Wk = np.asarray(inputs["L0_attn_Wk"])
    bk = np.asarray(inputs["L0_attn_bk"])
    Wv = np.asarray(inputs["L0_attn_Wv"])
    bv = np.asarray(inputs["L0_attn_bv"])
    bw = np.asarray(inputs["L0_attn_bias_weight"])
    Wb = np.asarray(inputs["L0_ffn_Wb"])
    fbias = np.asarray(inputs["L0_ffn_bias"])
    Wr = np.asarray(inputs["L0_ffn_Wr"])
    A = np.asarray(inputs["L0_ffn_A"])
    Bm = np.asarray(inputs["L0_ffn_Bm"])

    consts = {
        "biv3": _BIV3,
        "revI": _REVI,
        "diagI": _DIAGI,
        "idbf": np.eye(128, dtype=ml_dtypes.bfloat16),
        "idf32": np.eye(128, dtype=np.float32),
        "ones_r_bf": np.ones((1, 128), dtype=ml_dtypes.bfloat16),
        "ones_c_bf": np.ones((128, 1), dtype=ml_dtypes.bfloat16),
        "rrep": np.repeat(np.eye(8, dtype=np.float32), 16, axis=1),
    }
    WqT = np.ascontiguousarray(Wq.T)
    WkT = np.ascontiguousarray(Wk.T)
    WvT = np.ascontiguousarray(Wv.T)
    WbT = np.ascontiguousarray(Wb.T)
    WrT = np.ascontiguousarray(Wr.T)
    A_f = A.reshape(128, D)
    BmT_f = np.ascontiguousarray(Bm.transpose(0, 2, 1).reshape(128, D))

    in_maps = []
    perms = []
    for c in range(8):
        b, hh = c >> 1, c & 1
        pt = np.concatenate([np.arange(hh * 2048, (hh + 1) * 2048),
                             np.arange((1 - hh) * 2048, (2 - hh) * 2048)])
        pd = np.concatenate([np.arange(hh * 2048, (hh + 1) * 2048),
                             np.arange((1 - hh) * 2048, (2 - hh) * 2048)])
        perms.append(pd)
        fsl = slice(hh * 2048, (hh + 1) * 2048)
        m = dict(consts)
        m["xb"] = np.ascontiguousarray(x[b][pt][:, pd])
        m["wqT"] = np.ascontiguousarray(WqT[pd][:, fsl])
        m["wkT"] = np.ascontiguousarray(WkT[pd][:, fsl])
        m["wvT"] = np.ascontiguousarray(WvT[pd][:, fsl])
        m["wbT"] = np.ascontiguousarray(WbT[pd][:, pd])
        m["wrT"] = np.ascontiguousarray(WrT[pd])
        m["aT"] = np.ascontiguousarray(A_f.T[pd])
        m["bmT"] = np.ascontiguousarray(BmT_f[:, pd])
        m["fbias"] = np.ascontiguousarray(fbias[pd]).reshape(1, D)
        m["bq_c"] = np.ascontiguousarray(bq[fsl].reshape(16, 128).T)
        m["bk_c"] = np.ascontiguousarray(bk[fsl].reshape(16, 128).T)
        m["bv_r"] = np.ascontiguousarray(bv[fsl]).reshape(1, FH)
        m["bwT3"] = np.ascontiguousarray(bw.T[:, hh * 8:(hh + 1) * 8])
        m["sel0"] = np.full((128, 1), 1.0 if hh == 1 else 0.0, np.float32)
        m["sel1"] = np.full((128, 1), 0.0 if hh == 1 else 1.0, np.float32)
        in_maps.append(m)
    return in_maps, perms


def kernel(**inputs):
    nc = _get_nc()
    in_maps, perms = make_in_maps(inputs)
    res = run_bass_kernel_spmd(nc, in_maps, core_ids=list(range(8)))
    shards = []
    for c in range(8):
        shard = np.asarray(res.results[c]["out"])
        nat = np.empty_like(shard)
        nat[:, perms[c]] = shard
        shards.append(nat)
    full = np.concatenate(shards, axis=0)
    return full.reshape(4, 4096, D)


# version 18
# speedup vs baseline: 1.0488x; 1.0488x over previous
import numpy as np
import ml_dtypes
import concourse.bass as bass
import concourse.mybir as mybir
from concourse.bass_utils import run_bass_kernel_spmd
from contextlib import ExitStack

BF = mybir.dt.bfloat16
F32 = mybir.dt.float32
AF = mybir.ActivationFunctionType
AL = mybir.AluOpType
AX = mybir.AxisListType.X

D = 4096
T_ALL = 4096
T_OWN = 2048
FH = 2048
NHL = 8
SCALE = 1.0 / 16.0
EPS = 1e-5
T_PASS = 1024
TP3 = 512
N_PASS1 = T_ALL // T_PASS
N_PASS3 = T_OWN // TP3

DEBUG = False



def _build_tables():
    g = [1.0, 1.0, 1.0]
    dim, n = 3, 8
    masks = sorted(range(n), key=lambda m: (bin(m).count("1"), m))
    inv = {m: i for i, m in enumerate(masks)}
    C = np.zeros((n, n, n), dtype=np.float32)
    for i, a in enumerate(masks):
        for j, b in enumerate(masks):
            sw, at = (0, a >> 1)
            while at:
                sw += bin(at & b).count("1")
                at >>= 1
            s = -1.0 if sw & 1 else 1.0
            for t in range(dim):
                if a >> t & 1 and b >> t & 1:
                    s *= float(g[t])
            C[i, j, inv[a ^ b]] = s
    grades = np.array([bin(m).count("1") for m in masks])
    rev = np.where(grades * (grades - 1) // 2 % 2 == 1, -1.0, 1.0).astype(np.float32)
    return C, rev


_C_TAB, _REV = _build_tables()
_BIV3 = np.ascontiguousarray(
    _C_TAB[:, :, [4, 5, 6]].transpose(2, 0, 1).reshape(3, 64)
).astype(np.float32)
_REVI = np.repeat(_REV, 8).reshape(64, 1).astype(np.float32)
_DIAGI = np.zeros((64, 1), np.float32)
for _i in range(8):
    _DIAGI[_i * 8 + _i, 0] = _REV[_i] * SCALE

ENGS = ("PE", "DVE", "ACT", "SP", "PL")


class SemC:
    __slots__ = ("name", "count", "h")

    def __init__(self, name):
        self.name = name
        self.count = 0
        self.h = None

    def tick(self, n=1):
        self.count += n
        return self.count


class Plan:
    def __init__(self):
        self.q = {e: [] for e in ENGS}
        self.sems = []
        self.self_sem = {"DVE": self.sem("dve_self"), "ACT": self.sem("act_self")}
        self.last_evt = {"DVE": None, "ACT": None}

    def sem(self, name):
        s = SemC(name)
        self.sems.append(s)
        return s

    def op(self, eng, fn, wait=(), inc=None, selfser=True):
        ss = self.self_sem.get(eng) if selfser else None
        prev_evt = self.last_evt.get(eng) if ss is not None else None
        my_inc = inc
        if ss is not None:
            if my_inc is None:
                my_inc = (ss, 1)
                ss.tick()
            self.last_evt[eng] = (my_inc[0], my_inc[0].count)
        wait = tuple((sb, v) for sb, v in wait if v > 0)

        def run(e):
            for sb, v in wait:
                e.wait_ge(sb.h, v)
            if prev_evt is not None:
                e.wait_ge(prev_evt[0].h, prev_evt[1])
            ins = fn(e)
            if my_inc is not None:
                sb, n = my_inc
                ins.then_inc(sb.h, n)

        self.q[eng].append(run)

    def dma(self, eng, fn, wait=(), inc=None):
        self.op(eng, fn, wait=wait, inc=inc, selfser=False)

    def all_done(self, eng):
        ss = self.self_sem[eng]
        w = [(ss, ss.count)]
        ev = self.last_evt.get(eng)
        if ev is not None:
            w.append(ev)
        return w

    def barrier_waits(self):
        return [(s, s.count) for s in self.sems if s.count > 0]

    def barrier(self):
        w = self.barrier_waits()
        for eng in ENGS:
            self.op(eng, lambda e: e.nop(), wait=w, selfser=False)


def build_program():
    nc = bass.Bass("TRN2", target_bir_lowering=False)
    P = Plan()
    es0 = ExitStack()

    def din(name, shape, dt=F32):
        return nc.dram_tensor(name, list(shape), dt, kind="ExternalInput").ap()

    xb = din("xb", (T_ALL, D))
    wqT = din("wqT", (D, FH))
    wkT = din("wkT", (D, FH))
    wvT = din("wvT", (D, FH))
    wbT = din("wbT", (D, D))
    wrT = din("wrT", (D, 8))
    aT = din("aT", (D, 128))
    bmT = din("bmT", (128, D))
    fbias = din("fbias", (1, D))
    bq_c = din("bq_c", (128, 16))
    bk_c = din("bk_c", (128, 16))
    bv_r = din("bv_r", (1, FH))
    bwT3 = din("bwT3", (3, 8))
    biv3 = din("biv3", (3, 64))
    revI = din("revI", (64, 1))
    diagI = din("diagI", (64, 1))
    idbf = din("idbf", (128, 128), BF)
    idf32 = din("idf32", (128, 128))
    ones_r_bf = din("ones_r_bf", (1, 128), BF)
    ones_c_bf = din("ones_c_bf", (128, 1), BF)
    rrep = din("rrep", (8, 128))
    sel0in = din("sel0", (128, 1))
    sel1in = din("sel1", (128, 1))

    out = nc.dram_tensor("out", [T_OWN, D], F32, kind="ExternalOutput").ap()

    Qfm = nc.dram_tensor("Qfm", [FH, T_ALL], BF, kind="Internal").ap()
    Kfm = nc.dram_tensor("Kfm", [FH, T_ALL], BF, kind="Internal").ap()
    Vtok = nc.dram_tensor("Vtok", [T_ALL, FH], BF, kind="Internal").ap()
    attn_own = nc.dram_tensor("attn_own", [T_OWN, FH], F32, kind="Internal").ap()
    contribs = [nc.dram_tensor(f"contrib{k}", [T_OWN, 512], BF, kind="Internal").ap()
                for k in range(4)]
    gaths = [nc.dram_tensor(f"gath{k}", [2 * T_OWN, 512], BF, kind="Internal").ap()
             for k in range(4)]
    ntab = nc.dram_tensor("ntab", [8, 64], BF, kind="Internal").ap()
    x1d = nc.dram_tensor("x1d", [T_OWN, D], F32, kind="Internal").ap()

    dbg = {}
    if DEBUG:
        for nm, shp in (("q", (FH, T_ALL)), ("k", (FH, T_ALL))):
            dbg[nm] = nc.dram_tensor("dbg_" + nm, list(shp), BF,
                                     kind="ExternalOutput").ap()
        dbg["v"] = nc.dram_tensor("dbg_v", [T_ALL, FH], BF,
                                  kind="ExternalOutput").ap()
        dbg["attn"] = nc.dram_tensor("dbg_attn", [T_OWN, FH], F32,
                                     kind="ExternalOutput").ap()
        dbg["c"] = nc.dram_tensor("dbg_c", [NHL, 24], F32,
                                  kind="ExternalOutput").ap()
        dbg["pt"] = nc.dram_tensor("dbg_pt", [128, 1024], BF,
                                   kind="ExternalOutput").ap()
        dbg["kb"] = nc.dram_tensor("dbg_kb", [128, 2 * T_ALL], BF,
                                   kind="ExternalOutput").ap()
        dbg["lg"] = nc.dram_tensor("dbg_lg", [128, 1024], F32,
                                   kind="ExternalOutput").ap()
        dbg["qb"] = nc.dram_tensor("dbg_qb", [128, 2 * T_ALL], BF,
                                   kind="ExternalOutput").ap()
        dbg["vb"] = nc.dram_tensor("dbg_vb", [128, 32 * 257], BF,
                                   kind="ExternalOutput").ap()
        dbg["x1"] = nc.dram_tensor("dbg_x1", [T_OWN, D], F32,
                                   kind="ExternalOutput").ap()

    def sb(stack, name, shape, dt):
        return stack.enter_context(nc.sbuf_tensor(name, list(shape), dt))

    c_biv = sb(es0, "c_biv", (3, 64), F32)
    c_bw = sb(es0, "c_bw", (3, 8), F32)
    c_revI = sb(es0, "c_revI", (64, 1), F32)
    c_diagI = sb(es0, "c_diagI", (64, 1), F32)
    c_idbf = sb(es0, "c_idbf", (128, 128), BF)
    c_idf32 = sb(es0, "c_idf32", (128, 128), F32)
    c_ones_r = sb(es0, "c_ones_r", (1, 128), BF)
    c_ones_c = sb(es0, "c_ones_c", (128, 1), BF)
    c_rrep = sb(es0, "c_rrep", (8, 128), F32)
    c_sel0 = sb(es0, "c_sel0", (128, 1), F32)
    c_sel1 = sb(es0, "c_sel1", (128, 1), F32)
    c_bq = sb(es0, "c_bq", (128, 16), F32)
    c_bk = sb(es0, "c_bk", (128, 16), F32)
    c_bv_bf = sb(es0, "c_bv_bf", (1, FH), BF)
    c_fb_bf = sb(es0, "c_fb_bf", (1, D), BF)
    s_nt = sb(es0, "s_nt", (64, 8), F32)
    s_nt2 = sb(es0, "s_nt2", (64, 8), F32)
    s_nth = sb(es0, "s_nth", (8, 64), BF)
    def mk_ln_tiles(stack, tag, nx=2):
        xnt2 = [sb(stack, f"xnt{tag}{i}", (128, D), BF) for i in range(nx)]
        lns2 = [{k: sb(stack, f"ln{tag}{i}_" + k, (128, 1), F32) for k in
                 ("s1", "s2", "m", "b2", "mm", "var", "veps", "sq", "rstd",
                  "negmr")} for i in range(2)]
        return xnt2, lns2

    psA = es0.enter_context(nc.psum_tensor("psA", [128, 1024], F32))
    psB = es0.enter_context(nc.psum_tensor("psB", [128, 1024], F32))
    psC = es0.enter_context(nc.psum_tensor("psC", [128, 1024], F32))
    psD = es0.enter_context(nc.psum_tensor("psD", [128, 512], F32))
    psE = es0.enter_context(nc.psum_tensor("psE", [128, 512], F32))

    s_x = P.sem("s_x")
    s_w = P.sem("s_w")
    s_pe = P.sem("s_pe")
    s_dve = P.sem("s_dve")
    s_act = P.sem("s_act")
    s_sp = P.sem("s_sp")
    s_ld2 = P.sem("s_ld2")
    s_st = P.sem("s_st")
    s_x1 = P.sem("s_x1")
    s_xp = [P.sem("s_xp0"), P.sem("s_xp1")]
    s_wp = [P.sem("s_wp0"), P.sem("s_wp1")]
    s_misc = P.sem("s_misc")
    s_cc = [P.sem(f"s_cc{k}") for k in range(4)]

    ps_free = {"D": [], "E": []}

    for dst, src in ((c_biv, biv3), (c_bw, bwT3), (c_revI, revI),
                     (c_diagI, diagI), (c_idbf, idbf), (c_idf32, idf32),
                     (c_ones_r, ones_r_bf), (c_ones_c, ones_c_bf),
                     (c_rrep, rrep), (c_sel0, sel0in), (c_sel1, sel1in),
                     (c_bq, bq_c), (c_bk, bk_c)):
        s_x.tick(16)
        P.dma("SP", lambda e, d=dst, s=src: e.dma_start(d[:, :], s),
              inc=(s_x, 16))
    s_misc.tick(16)
    P.dma("PL", lambda e: e.dma_start(c_bv_bf[:, :], bv_r), inc=(s_misc, 16))
    s_misc.tick(16)
    P.dma("PL", lambda e: e.dma_start(c_fb_bf[:, :], fbias), inc=(s_misc, 16))
    rows_tok = s_misc.count
    ld_all = s_x.count

    pe1 = s_pe.tick()
    P.op("PE", lambda e: e.matmul(psD[:64, 0:8], c_biv[:, :], c_bw[:, :],
                                  start=True, stop=True),
         wait=[(s_x, ld_all)], inc=(s_pe, 1), selfser=False)
    d_nt0 = s_dve.tick()
    P.op("DVE", lambda e: e.tensor_scalar_mul(s_nt[:, :], psD[:64, 0:8],
                                              c_revI[:, 0:1]),
         wait=[(s_pe, pe1)], inc=(s_dve, 1))
    ps_free["D"] = [(s_dve, d_nt0)]
    d1 = s_dve.tick()
    P.op("DVE", lambda e: e.tensor_scalar_add(s_nt2[:, :], s_nt[:, :],
                                              c_diagI[:, 0:1]), inc=(s_dve, 1))
    pe2 = s_pe.tick()
    P.op("PE", lambda e: e.matmul(psE[:8, 0:64], s_nt2[:, :], c_idf32[:64, :64],
                                  start=True, stop=True),
         wait=[(s_dve, d1)], inc=(s_pe, 1), selfser=False)
    d2 = s_dve.tick()
    P.op("DVE", lambda e: e.tensor_copy(s_nth[:, :], psE[:8, 0:64]),
         wait=[(s_pe, pe2)], inc=(s_dve, 1))
    ps_free["E"] = [(s_dve, d2)]
    t_nt = s_sp.tick(16)
    P.dma("PL", lambda e: e.dma_start(ntab, s_nth[:, :]),
          wait=[(s_dve, d2)], inc=(s_sp, 16))

    es1 = ExitStack()
    xnT = sb(es1, "xnT", (128, 32 * T_PASS), BF)
    W1 = 256
    wbuf = [sb(es1, f"wbuf{i}", (128, 32 * W1), BF) for i in range(2)]
    xt = [sb(es1, f"xt{i}", (128, D), F32) for i in range(2)]
    qraw = sb(es1, "qraw", (128, T_PASS), BF)
    evb = [sb(es1, f"evb{i}", (128, T_PASS), BF) for i in range(2)]
    vevb = [sb(es1, f"vevb{i}", (128, 512), BF) for i in range(2)]
    btiles = sb(es1, "btiles", (128, NHL * 128), BF)
    xnt2, lns2 = mk_ln_tiles(es1, "p1", nx=4)
    pdump = sb(es1, "pdump1", (128, D), BF)

    dz = s_dve.tick()
    P.op("DVE", lambda e: e.memset(btiles[:, :], 0.0), inc=(s_dve, 1))
    for h in range(NHL):
        for db in range(16):
            s_sp.tick(16)
            P.dma("PL",
                  lambda e, h=h, db=db: e.dma_start(
                      btiles[db * 8:db * 8 + 8,
                             h * 128 + db * 8: h * 128 + db * 8 + 8],
                      ntab[h:h + 1, :].rearrange("1 (i j) -> i j", i=8)),
                  wait=[(s_sp, t_nt), (s_dve, dz)], inc=(s_sp, 16))
    btiles_tok = s_sp.count

    def ln_sums(xtile, ready, lns, dump):
        a1 = s_act.tick()
        P.op("ACT", lambda e: e.activation(dump[:, :], xtile[:, :],
                                           AF.Identity,
                                           accum_out=lns["s1"][:, :]),
             wait=ready, inc=(s_act, 1))
        a2 = s_act.tick()
        P.op("ACT", lambda e: e.activation(dump[:, :], xtile[:, :], AF.Square,
                                           accum_out=lns["s2"][:, :]),
             inc=(s_act, 1))
        return a2

    def ln_finish(xtile, a2, lns, dst_bf):
        P.op("DVE", lambda e: e.tensor_single_scalar(lns["m"][:, :],
                                                     lns["s1"][:, :],
                                                     1.0 / D, AL.mult),
             wait=[(s_act, a2)])
        st_tok = s_dve.tick()
        P.op("DVE", lambda e: e.tensor_single_scalar(lns["b2"][:, :],
                                                     lns["s2"][:, :],
                                                     1.0 / D, AL.mult),
             inc=(s_dve, 1))
        P.op("DVE", lambda e: e.tensor_mul(lns["mm"][:, :], lns["m"][:, :],
                                           lns["m"][:, :]))
        P.op("DVE", lambda e: e.tensor_sub(lns["var"][:, :], lns["b2"][:, :],
                                           lns["mm"][:, :]))
        dv = s_dve.tick()
        P.op("DVE", lambda e: e.tensor_single_scalar(lns["veps"][:, :],
                                                     lns["var"][:, :], EPS,
                                                     AL.add), inc=(s_dve, 1))
        a3 = s_act.tick()
        P.op("ACT", lambda e: e.activation(lns["sq"][:, :], lns["veps"][:, :],
                                           AF.Sqrt),
             wait=[(s_dve, dv)], inc=(s_act, 1))
        P.op("DVE", lambda e: e.reciprocal(lns["rstd"][:, :], lns["sq"][:, :]),
             wait=[(s_act, a3)])
        P.op("DVE", lambda e: e.tensor_scalar(lns["negmr"][:, :], lns["m"][:, :],
                                              lns["rstd"][:, 0:1], -1.0,
                                              AL.mult, AL.mult))
        dn = s_dve.tick()
        P.op("DVE", lambda e: e.tensor_scalar(dst_bf[:, :], xtile[:, :],
                                              lns["rstd"][:, 0:1],
                                              lns["negmr"][:, 0:1],
                                              AL.mult, AL.add), inc=(s_dve, 1))
        return dn, st_tok

    def emit_transposes(src_bf, src_tok, toff, xnT_t):
        last_a = 0
        last_d = 0
        for grp in range(8):
            bank = "D" if grp % 2 == 0 else "E"
            pst = psD if grp % 2 == 0 else psE
            pe_t = s_pe.tick()
            for k in range(4):
                dc = grp * 4 + k

                def mmt(e, dc=dc, k=k, pst=pst):
                    return e.matmul(pst[:, k * 128:(k + 1) * 128],
                                    src_bf[:, dc * 128:(dc + 1) * 128],
                                    c_idbf[:, :], start=True, stop=True)
                w = ()
                if k == 0:
                    w = ps_free[bank] + [(s_dve, src_tok)]
                P.op("PE", mmt, wait=w,
                     inc=(s_pe, 1) if k == 3 else None, selfser=False)

            def evac(e, grp=grp, pst=pst, toff=toff, xnT_t=xnT_t):
                dst = xnT_t[:, :].rearrange("p (dc q) -> p dc q", dc=32)
                d = dst[:, grp * 4:(grp + 1) * 4, toff:toff + 128]
                if grp % 2 == 1:
                    return e.copy(d, pst[:, 0:512])
                return e.tensor_copy(d, pst[:, 0:512])
            if grp % 2 == 1:
                last_a = s_act.tick()
                P.op("ACT", evac, wait=[(s_pe, pe_t)], inc=(s_act, 1))
                ps_free[bank] = [(s_act, last_a)]
            else:
                last_d = s_dve.tick()
                P.op("DVE", evac, wait=[(s_pe, pe_t)], inc=(s_dve, 1))
                ps_free[bank] = [(s_dve, last_d)]
        return [(s_act, last_a), (s_dve, last_d)]

    def emit_wload(src_ap, col0, buf, wbuf_t, extra_wait=(), width=512):
        sem = s_wp[buf % 2]
        for part in range(4):
            sem.tick(16)

            def ld(e, part=part, col0=col0, buf=buf, src_ap=src_ap,
                   wbuf_t=wbuf_t, width=width):
                src = src_ap.rearrange("(dc p) f -> p dc f", p=128)
                dst = wbuf_t[buf][:, :].rearrange("p (dc f) -> p dc f",
                                                  f=width)
                return e.dma_start(dst[:, part * 8:(part + 1) * 8, :],
                                   src[:, part * 8:(part + 1) * 8,
                                       col0:col0 + width])
            P.dma("PL", ld, wait=extra_wait if part == 0 else (),
                  inc=(sem, 16))
        return (sem, sem.count)

    x_free = [0, 0]
    ln_free = [0, 0]
    wq_tok = emit_wload(wqT, 0, 0, wbuf, width=W1)
    pe_buf_use = {0: 0, 1: 0}
    tr_tok = []
    ln_sums_tok = {}
    ln_dn = {}

    def p1_load_sums(p, i):
        t0 = p * T_PASS
        bi = i % 2
        tokx = s_xp[bi].tick(16)
        P.dma("SP", lambda e, t0=t0, i=i, bi=bi: e.dma_start(
            xt[bi][:, :], xb[t0 + i * 128: t0 + (i + 1) * 128, :]),
            wait=[(s_dve, x_free[bi])], inc=(s_xp[bi], 16))
        ln_sums_tok[i] = ln_sums(xt[bi],
                                 [(s_xp[bi], tokx), (s_dve, ln_free[bi])],
                                 lns2[bi], pdump)

    def p1_finish(p, i):
        bi = i % 2
        dn, st_t = ln_finish(xt[bi], ln_sums_tok[i], lns2[bi], xnt2[i % 4])
        x_free[bi] = dn
        ln_free[bi] = st_t
        ln_dn[i] = dn

    def p1_transp(p, i):
        return emit_transposes(xnt2[i % 4], ln_dn[i], i * 128, xnT)

    def next_pass_drains(np_):
        return {0: [lambda: p1_load_sums(np_, 0)],
                1: [lambda: p1_load_sums(np_, 1)],
                2: [lambda: p1_finish(np_, 0)],
                3: [lambda: p1_load_sums(np_, 2)],
                4: [lambda: p1_finish(np_, 1)],
                5: [lambda: p1_load_sums(np_, 3)],
                6: [lambda: p1_finish(np_, 2)],
                7: [lambda: p1_finish(np_, 3)],
                8: [lambda: p1_load_sums(np_, 4)],
                9: [lambda: p1_load_sums(np_, 5)]}

    def next_pass_tail(np_):
        global_tr = None
        for i in range(4):
            global_tr = p1_transp(np_, i)
        p1_finish(np_, 4)
        global_tr = p1_transp(np_, 4)
        p1_finish(np_, 5)
        global_tr = p1_transp(np_, 5)
        p1_load_sums(np_, 6)
        p1_finish(np_, 6)
        global_tr = p1_transp(np_, 6)
        p1_load_sums(np_, 7)
        p1_finish(np_, 7)
        global_tr = p1_transp(np_, 7)
        return global_tr

    for i in range(2):
        p1_load_sums(0, i)
    p1_finish(0, 0)
    p1_load_sums(0, 2)
    p1_finish(0, 1)
    p1_load_sums(0, 3)
    p1_finish(0, 2)
    p1_finish(0, 3)
    p1_load_sums(0, 4)
    p1_load_sums(0, 5)
    tr_tok = next_pass_tail(0)

    for p in range(N_PASS1):
        t0 = p * T_PASS
        drains = next_pass_drains(p + 1) if p + 1 < N_PASS1 else {}
        nblk = FH // W1
        blocks = ([("Q", wqT, k) for k in range(nblk)] +
                  [("K", wkT, k) for k in range(nblk)] +
                  [("V", wvT, k) for k in range(nblk)])
        wtoks = {0: wq_tok}
        ev_toks = {}
        for bidx, (mat, wap, kblk) in enumerate(blocks):
            buf = bidx % 2
            for fn_ in drains.get(bidx, ()):
                fn_()
            if bidx + 1 < len(blocks):
                _, nap, nk = blocks[bidx + 1]
                wtoks[bidx + 1] = emit_wload(
                    nap, nk * W1, (bidx + 1) % 2, wbuf,
                    extra_wait=[(s_pe, pe_buf_use[(bidx + 1) % 2])],
                    width=W1)
            if mat in ("Q", "K"):
                for fc4 in range(W1 // 128):
                    fc = kblk * (W1 // 128) + fc4
                    pacc = psA if fc % 2 == 0 else psB
                    pe_tok = s_pe.tick()
                    for dc in range(32):
                        for g2 in range(2):

                            def mm(e, dc=dc, g2=g2, fc4=fc4, buf=buf, pacc=pacc):
                                ws = wbuf[buf][:, :].rearrange(
                                    "p (dc f) -> p dc f", f=W1)
                                return e.matmul(
                                    pacc[:, g2 * 512:(g2 + 1) * 512],
                                    ws[:, dc, fc4 * 128:(fc4 + 1) * 128],
                                    xnT[:, dc * T_PASS + g2 * 512:
                                        dc * T_PASS + (g2 + 1) * 512],
                                    start=(dc == 0), stop=(dc == 31))
                            w = ()
                            if dc == 0 and g2 == 0:
                                w = ([wtoks[bidx],
                                      (s_act, ev_toks.get(("qk", fc % 2), 0)),
                                      (s_dve, ev_toks.get(("fold", fc % 2), 0))]
                                     + tr_tok)
                            P.op("PE", mm, wait=w,
                                 inc=(s_pe, 1) if (dc == 31 and g2 == 1) else None,
                                 selfser=False)
                    pe_buf_use[buf] = pe_tok
                    bias_col = c_bq if mat == "Q" else c_bk
                    dst = qraw if mat == "Q" else evb[fc % 2]
                    a_ev = s_act.tick()
                    P.op("ACT", lambda e, pacc=pacc, dst=dst, bias_col=bias_col,
                         fc=fc: e.add(dst[:, :], pacc[:, :],
                                      bias_col[:, fc:fc + 1]),
                         wait=[(s_pe, pe_tok), (s_sp, s_sp.count)],
                         inc=(s_act, 1))
                    ev_toks[("qk", fc % 2)] = a_ev
                    if mat == "Q":
                        hloc = fc // 2
                        pe_f = s_pe.tick()
                        for g2 in range(2):

                            def mf(e, g2=g2, hloc=hloc):
                                return e.matmul(
                                    psC[:, g2 * 512:(g2 + 1) * 512],
                                    btiles[:, hloc * 128:(hloc + 1) * 128],
                                    qraw[:, g2 * 512:(g2 + 1) * 512],
                                    start=True, stop=True)
                            w = ()
                            if g2 == 0:
                                w = [(s_act, a_ev), (s_sp, btiles_tok),
                                     (s_dve, ev_toks.get("foldC", 0))]
                            P.op("PE", mf, wait=w,
                                 inc=(s_pe, 1) if g2 == 1 else None,
                                 selfser=False)
                        d_f = s_dve.tick()
                        P.op("DVE", lambda e, fc=fc: e.tensor_copy(
                            evb[fc % 2][:, :], psC[:, :]),
                            wait=[(s_pe, pe_f), (s_sp, s_sp.count)],
                            inc=(s_dve, 1))
                        ev_toks["foldC"] = d_f
                        ev_toks[("fold", fc % 2)] = d_f
                        src_sp = d_f
                        dstap = Qfm
                    else:
                        src_sp = None
                        dstap = Kfm
                    sp_tok = s_sp.tick(16)
                    P.dma("PL", lambda e, fc=fc, t0=t0, dstap=dstap:
                          e.dma_start(dstap[fc * 128:(fc + 1) * 128,
                                            t0:t0 + T_PASS], evb[fc % 2][:, :]),
                          wait=([(s_dve, src_sp)] if src_sp is not None
                                else [(s_act, a_ev)]),
                          inc=(s_sp, 16))
                    ev_toks[("spill", fc % 2)] = sp_tok
            else:
                for i in range(8):
                    pst = psD if i % 2 == 0 else psE
                    pe_tok = s_pe.tick()
                    for dc in range(32):

                        def mv(e, dc=dc, i=i, buf=buf, pst=pst):
                            ws = wbuf[buf][:, :].rearrange(
                                "p (dc f) -> p dc f", f=W1)
                            return e.matmul(
                                pst[:, 0:W1],
                                xnT[:, dc * T_PASS + i * 128:
                                    dc * T_PASS + (i + 1) * 128],
                                ws[:, dc, :], start=(dc == 0), stop=False)
                        w = ()
                        if dc == 0:
                            w = ([wtoks[bidx], (s_misc, rows_tok)]
                                 + ps_free["D" if i % 2 == 0 else "E"]
                                 + tr_tok)
                        P.op("PE", mv, wait=w, selfser=False)
                    P.op("PE", lambda e, kblk=kblk, pst=pst: e.matmul(
                        pst[:, 0:W1], c_ones_r[:, :],
                        c_bv_bf[0:1, kblk * W1:(kblk + 1) * W1],
                        start=False, stop=True),
                        inc=(s_pe, 1), selfser=False)
                    pe_buf_use[buf] = pe_tok
                    d_v = s_dve.tick()
                    P.op("DVE", lambda e, i=i, pst=pst: e.tensor_copy(
                        vevb[i % 2][:, 0:W1], pst[:, 0:W1]),
                        wait=[(s_pe, pe_tok), (s_sp, s_sp.count)],
                        inc=(s_dve, 1))
                    ev_toks[("v", i % 2)] = d_v
                    ps_free["D" if i % 2 == 0 else "E"] = [(s_dve, d_v)]
                    vsp_tok = s_sp.tick(16)
                    P.dma("PL", lambda e, i=i, kblk=kblk, t0=t0: e.dma_start(
                        Vtok[t0 + i * 128: t0 + (i + 1) * 128,
                             kblk * W1:(kblk + 1) * W1],
                        vevb[i % 2][:, 0:W1]),
                        wait=[(s_dve, d_v)], inc=(s_sp, 16))
                    ev_toks[("vspill", i % 2)] = vsp_tok
        if p + 1 < N_PASS1:
            wq_tok = emit_wload(wqT, 0, 0, wbuf,
                                extra_wait=[(s_pe, pe_buf_use[0])], width=W1)
            wtoks = None
            tr_tok = next_pass_tail(p + 1)
    if DEBUG:
        for (ap_s, ap_d, rows) in ((Qfm, dbg["q"], FH), (Kfm, dbg["k"], FH),
                                   (Vtok, dbg["v"], T_ALL)):
            for r0 in range(0, rows, 128):
                s_st.tick(16)
                P.dma("PL", lambda e, ap_s=ap_s, ap_d=ap_d, r0=r0: e.dma_start(
                    ap_d[r0:r0 + 128, :], ap_s[r0:r0 + 128, :]),
                    wait=[(s_sp, s_sp.count)], inc=(s_st, 16))

    es1.close()
    P.barrier()

    es2 = ExitStack()
    kbuf = [sb(es2, f"kbuf{i}", (128, 2 * T_ALL), BF) for i in range(2)]
    qbuf = [sb(es2, f"qbuf{i}", (128, 2 * T_ALL), BF) for i in range(2)]
    vbuf = [sb(es2, f"vbuf{i}", (128, 32 * 257), BF) for i in range(2)]
    probsT = sb(es2, "probsT", (128, 32 * 1024), BF)
    csq = sb(es2, "csq", (128, 2 * T_ALL), BF)
    att = [sb(es2, f"att{i}", (128, 256), F32) for i in range(2)]
    c2q = sb(es2, "c2q", (1, 16), F32)
    cC = {k: sb(es2, "cc_" + k, (1, 1), F32) for k in ("mq", "mk", "p", "ng")}
    cCb = sb(es2, "cc_ngb", (1, 1), BF)
    negCb = [sb(es2, f"negC{i}", (128, 1), F32) for i in range(2)]
    rc2 = sb(es2, "rc2", (128, 1), F32)
    dbg_lgs = sb(es2, "dbg_lgs", (128, 1024), F32) if DEBUG else None

    for b_ in range(2):
        s_dve.tick()
        P.op("DVE", lambda e, b_=b_: e.memset(
            vbuf[b_][:, :].rearrange("p (mc f) -> p mc f", f=257)[:, :, 256:257],
            1.0), inc=(s_dve, 1))
    vones_tok = s_dve.count

    def p2_loads(h, bi):
        guard = [(s_pe, s_pe.count)] + P.all_done("ACT")
        for fc in range(2):
            s_ld2.tick(16)
            P.dma("SP", lambda e, h=h, fc=fc, bi=bi: e.dma_start(
                kbuf[bi][:, fc * T_ALL:(fc + 1) * T_ALL],
                Kfm[h * 256 + fc * 128: h * 256 + fc * 128 + 128, :]),
                wait=guard if fc == 0 else (), inc=(s_ld2, 16))
            s_ld2.tick(16)
            P.dma("SP", lambda e, h=h, fc=fc, bi=bi: e.dma_start(
                qbuf[bi][:, fc * T_ALL:(fc + 1) * T_ALL],
                Qfm[h * 256 + fc * 128: h * 256 + fc * 128 + 128, :]),
                inc=(s_ld2, 16))
        for part in range(4):
            s_ld2.tick(16)

            def lv(e, h=h, part=part, bi=bi):
                vs = vbuf[bi][:, :].rearrange("p (mc f) -> p mc f", f=257)
                src = Vtok.rearrange("(mc p) f -> p mc f", p=128)
                return e.dma_start(vs[:, part * 8:(part + 1) * 8, 0:256],
                                   src[:, part * 8:(part + 1) * 8,
                                       h * 256:(h + 1) * 256])
            P.dma("SP", lv, wait=[(s_dve, vones_tok)] if part == 0 else (),
                  inc=(s_ld2, 16))
        return s_ld2.count

    ld_tok = {0: p2_loads(0, 0), 1: p2_loads(1, 1)}
    exp_free = {}
    av_free = {}
    att_free = {}

    def emit_cbound_squares(h, which):
        bi = h % 2
        bufs = qbuf if which == "q" else kbuf
        pe_before = s_pe.count
        a2 = None
        for fc in range(2):
            a2 = s_act.tick()
            P.op("ACT", lambda e, bufs=bufs, bi=bi, fc=fc: e.activation(
                csq[:, fc * T_ALL:(fc + 1) * T_ALL],
                bufs[bi][:, fc * T_ALL:(fc + 1) * T_ALL], AF.Square),
                wait=([(s_ld2, ld_tok[h]), (s_pe, pe_before)]
                      if fc == 0 else ()),
                inc=(s_act, 1))
        return a2

    def emit_cbound(h, sq_q=None):
        bi = h % 2
        negC = negCb[h % 2]
        for which, bufs in (("q", qbuf), ("k", kbuf)):
            base = 0 if which == "q" else 8
            if which == "q" and sq_q is not None:
                a2 = sq_q
            else:
                a2 = emit_cbound_squares(h, which)
            for lg in range(8):
                pst = psD if lg % 2 == 0 else psE
                pe_t = s_pe.tick()
                bank = "D" if lg % 2 == 0 else "E"
                for fc in range(2):

                    def msum(e, lg=lg, fc=fc, pst=pst):
                        return e.matmul(
                            pst[0:1, :], c_ones_c[:, :],
                            csq[:, fc * T_ALL + lg * 512:
                                fc * T_ALL + (lg + 1) * 512],
                            start=(fc == 0), stop=(fc == 1))
                    w = ([(s_act, a2)] + ps_free[bank]) if fc == 0 else ()
                    P.op("PE", msum, wait=w,
                         inc=(s_pe, 1) if fc == 1 else None, selfser=False)
                d_mx = s_dve.tick()
                P.op("DVE", lambda e, lg=lg, pst=pst, base=base: e.reduce_max(
                    c2q[:, base + lg:base + lg + 1], pst[0:1, :], axis=AX),
                    wait=[(s_pe, pe_t)], inc=(s_dve, 1))
                ps_free[bank] = [(s_dve, d_mx)]
        P.op("DVE", lambda e: e.reduce_max(cC["mq"][:, :], c2q[:, 0:8], axis=AX))
        P.op("DVE", lambda e: e.reduce_max(cC["mk"][:, :], c2q[:, 8:16], axis=AX))
        dp = s_dve.tick()
        P.op("DVE", lambda e: e.tensor_mul(cC["p"][:, :], cC["mq"][:, :],
                                           cC["mk"][:, :]), inc=(s_dve, 1))
        a_rt = s_act.tick()
        P.op("ACT", lambda e: e.activation(cC["ng"][:, :], cC["p"][:, :],
                                           AF.Sqrt),
             wait=[(s_dve, dp)], inc=(s_act, 1))
        d_ng = s_dve.tick()
        P.op("DVE", lambda e: e.tensor_single_scalar(cCb[:, :], cC["ng"][:, :],
                                                     -1.0, AL.mult),
             wait=[(s_act, a_rt)], inc=(s_dve, 1))
        pe_b = s_pe.tick()
        P.op("PE", lambda e, negC=negC: e.matmul(
            psD[:, 0:1], c_ones_r[:, :], cCb[:, :], start=True, stop=True),
             wait=[(s_dve, d_ng)] + ps_free["D"], inc=(s_pe, 1), selfser=False)
        d_nc = s_dve.tick()
        P.op("DVE", lambda e, negC=negC: e.tensor_copy(negC[:, :], psD[:, 0:1]),
             wait=[(s_pe, pe_b)] + P.all_done("ACT"), inc=(s_dve, 1))
        ps_free["D"] = [(s_dve, d_nc)]
        return d_nc

    d_nc_tok = {0: None}
    d_nc_tok[0] = emit_cbound(0)
    for h in range(NHL):
        bi = h % 2
        negC = negCb[h % 2]
        d_nc = d_nc_tok[h]

        if DEBUG and h == 0:
            for nm, buf in (("kb", kbuf[0]), ("qb", qbuf[0]), ("vb", vbuf[0])):
                s_st.tick(16)
                P.dma("PL", lambda e, nm=nm, buf=buf: e.dma_start(
                    dbg[nm][:, :], buf[:, :]),
                    wait=[(s_ld2, ld_tok[0])], inc=(s_st, 16))
        if DEBUG:
            s_st.tick(16)
            P.dma("PL", lambda e, h=h: e.dma_start(
                dbg["c"][h:h + 1, 0:16], c2q[:, :]),
                wait=[(s_dve, d_nc)], inc=(s_st, 16))
            s_st.tick(16)
            P.dma("PL", lambda e, h=h: e.dma_start(
                dbg["c"][h:h + 1, 16:17], negC[0:1, 0:1]),
                inc=(s_st, 16))

        for qi, lq in enumerate((2, 3, 0, 1)):
            if qi == 1 and h + 1 < NHL:
                sqq_next = emit_cbound_squares(h + 1, "q")
            if qi == 2 and h + 1 < NHL:
                d_nc_tok[h + 1] = emit_cbound(h + 1, sq_q=sqq_next)
            l0 = lq * 1024
            for mc in range(32):
                pacc = (psA, psB, psC)[mc % 3]
                pe_t = s_pe.tick()
                for fc in range(2):
                    for g2 in range(2):

                        def mqk(e, mc=mc, fc=fc, g2=g2, bi=bi, l0=l0,
                                pacc=pacc):
                            return e.matmul(
                                pacc[:, g2 * 512:(g2 + 1) * 512],
                                kbuf[bi][:, fc * T_ALL + mc * 128:
                                         fc * T_ALL + (mc + 1) * 128],
                                qbuf[bi][:, fc * T_ALL + l0 + g2 * 512:
                                         fc * T_ALL + l0 + (g2 + 1) * 512],
                                start=(fc == 0), stop=(fc == 1))
                        w = ()
                        if fc == 0 and g2 == 0:
                            w = [(s_dve, d_nc), (s_act, exp_free.get(mc % 3, 0))]
                            pass
                        P.op("PE", mqk, wait=w,
                             inc=(s_pe, 1) if (fc == 1 and g2 == 1) else None,
                             selfser=False)
                if False and DEBUG and h == 0 and lq == 0 and mc == 0:
                    d_lg = s_dve.tick()
                    P.op("DVE", lambda e, pacc=pacc: e.tensor_copy(
                        dbg_lgs[:, :], pacc[:, :]),
                        wait=[(s_pe, pe_t)], inc=(s_dve, 1))
                    s_st.tick(16)
                    P.dma("PL", lambda e: e.dma_start(dbg["lg"][:, :],
                                                      dbg_lgs[:, :]),
                          wait=[(s_dve, d_lg)], inc=(s_st, 16))
                a_e = s_act.tick()
                P.op("ACT", lambda e, mc=mc, pacc=pacc, negC=negC:
                     e.activation(
                         probsT[:, mc * 1024:(mc + 1) * 1024], pacc[:, :],
                         AF.Exp, bias=negC[:, 0:1], scale=1.0),
                    wait=[(s_pe, pe_t)], inc=(s_act, 1))
                exp_free[mc % 3] = a_e
            exp_all = s_act.count
            if DEBUG and h == 0 and lq == 0:
                s_st.tick(16)
                P.dma("PL", lambda e: e.dma_start(
                    dbg["pt"][:, :], probsT[:, 0:1024]),
                    wait=[(s_act, exp_all)], inc=(s_st, 16))
            for lc in range(8):
                bank = "D" if lc % 2 == 0 else "E"
                pav = psD if lc % 2 == 0 else psE
                pe_t = s_pe.tick()
                for mc in range(32):

                    def mav(e, mc=mc, lc=lc, bi=bi, pav=pav):
                        return e.matmul(
                            pav[:, 0:257],
                            probsT[:, mc * 1024 + lc * 128:
                                   mc * 1024 + lc * 128 + 128],
                            vbuf[bi][:, mc * 257:(mc + 1) * 257],
                            start=(mc == 0), stop=(mc == 31))
                    w = ()
                    if mc == 0:
                        w = [(s_act, exp_all)] + ps_free[bank]
                    P.op("PE", mav, wait=w,
                         inc=(s_pe, 1) if mc == 31 else None, selfser=False)
                P.op("DVE", lambda e, pav=pav: e.reciprocal(
                    rc2[:, :], pav[:, 256:257]), wait=[(s_pe, pe_t)])
                d_at = s_dve.tick()
                P.op("DVE", lambda e, pav=pav, lc=lc: e.tensor_scalar_mul(
                    att[lc % 2][:, :], pav[:, 0:256], rc2[:, 0:1]),
                    wait=[(s_sp, s_sp.count)], inc=(s_dve, 1))
                ps_free[bank] = [(s_dve, d_at)]
                lg_ = l0 + lc * 128
                sp_t = s_sp.tick(16)
                if lg_ < T_OWN:
                    P.dma("PL", lambda e, lg_=lg_, h=h, lc=lc: e.dma_start(
                        attn_own[lg_:lg_ + 128, h * 256:(h + 1) * 256],
                        att[lc % 2][:, :]),
                        wait=[(s_dve, d_at)], inc=(s_sp, 16))
                else:
                    k = h // 2
                    P.dma("PL", lambda e, lg_=lg_, h=h, lc=lc, k=k: e.dma_start(
                        contribs[k][lg_ - T_OWN:lg_ - T_OWN + 128,
                                    (h % 2) * 256:(h % 2) * 256 + 256],
                        att[lc % 2][:, :]),
                        wait=[(s_dve, d_at)], inc=(s_sp, 16))
                att_free[lc % 2] = sp_t
        if h % 2 == 1:
            k = h // 2
            s_cc[k].tick()
            P.dma("PL", lambda e, k=k: e.collective_compute(
                "AllGather", AL.bypass,
                replica_groups=[[0, 1], [2, 3], [4, 5], [6, 7]],
                ins=[contribs[k].opt()], outs=[gaths[k].opt()]),
                wait=[(s_sp, s_sp.count)], inc=(s_cc[k], 1))
        if h + 2 < NHL:
            ld_tok[h + 2] = p2_loads(h + 2, (h + 2) % 2)

    if DEBUG:
        for r0 in range(0, T_OWN, 128):
            s_st.tick(16)
            P.dma("PL", lambda e, r0=r0: e.dma_start(
                dbg["attn"][r0:r0 + 128, :], attn_own[r0:r0 + 128, :]),
                wait=[(s_sp, s_sp.count)], inc=(s_st, 16))

    es2.close()
    P.barrier()

    es3 = ExitStack()
    xnT3 = sb(es3, "xnT3", (128, 32 * TP3), BF)
    wbuf3 = [sb(es3, f"wbuf3{i}", (128, 32 * 512), BF) for i in range(2)]
    xt3 = sb(es3, "xt3", (128, D), F32)
    aot = sb(es3, "aot", (128, FH), F32)
    gcat = sb(es3, "gcat", (128, D), BF)
    x1full = sb(es3, "x1full", (128, D), F32)
    sibscr = aot
    abuf = sb(es3, "abuf", (128, 32 * 128), BF)
    bmb = sb(es3, "bmb", (128, D), BF)
    wrb = sb(es3, "wrb", (128, 32 * 8), BF)
    w8 = sb(es3, "w8", (128, 8), F32)
    xnt2, lns2 = mk_ln_tiles(es3, "p3")
    r8 = {}
    for k in ("m1", "m2", "nm", "ssum", "rc"):
        r8[k] = sb(es3, "r8_" + k, (128, 1), F32)
    for k in ("eq", "msk", "ke", "ee", "wts"):
        r8[k] = sb(es3, "r8_" + k, (128, 8), F32)
    wT_sb = sb(es3, "wT_sb", (8, TP3), F32)
    h_sb = sb(es3, "h_sb", (128, TP3), BF)
    g_sb = sb(es3, "g_sb", (128, TP3), BF)
    OBW = 512
    x1tt = [sb(es3, f"x1tt{i}", (128, OBW), F32) for i in range(2)]
    outt = [sb(es3, f"outt{i}", (128, OBW), F32) for i in range(2)]

    def psR():
        return psB[:, 0:8]

    def psW():
        return psC[0:8, 0:128]

    for part in range(2):
        s_misc.tick(16)
        P.dma("PL", lambda e, part=part: e.dma_start(
            abuf[:, :].rearrange("p (dc f) -> p dc f", f=128)
            [:, part * 16:(part + 1) * 16, :],
            aT.rearrange("(dc p) f -> p dc f", p=128)
            [:, part * 16:(part + 1) * 16, :]), inc=(s_misc, 16))
    s_misc.tick(16)
    P.dma("PL", lambda e: e.dma_start(bmb[:, :], bmT), inc=(s_misc, 16))
    s_misc.tick(16)
    P.dma("PL", lambda e: e.dma_start(
        wrb[:, :].rearrange("p (dc f) -> p dc f", f=8)[:, :, :],
        wrT.rearrange("(dc p) f -> p dc f", p=128)[:, :, :]), inc=(s_misc, 16))
    tok_a = tok_bm = tok_r = s_misc.count

    xa_free = 0
    sums3_tok = 0
    ln_free3 = [0, 0]
    x1_free = 0
    x1sp_free = 0
    tr_tok = []
    wT_tok = 0
    psr_free = []
    pe_buf3 = {0: 0, 1: 0}
    wt_next = None
    x1_rot = [0]
    x1_busy = [0, 0]
    st_seen = [0]
    x1sp_pass = {}
    for p in range(N_PASS3):
        t0 = p * TP3
        if wt_next is None:
            wt_next = emit_wload(wbT, 0, 0, wbuf3,
                                 extra_wait=[(s_pe, pe_buf3[0])], width=OBW)
        for i in range(TP3 // 128):
            tc = t0 + i * 128
            s_x.tick(16)
            P.dma("SP", lambda e, tc=tc: e.dma_start(
                xt3[:, :], xb[tc:tc + 128, :]),
                wait=[(s_dve, xa_free), (s_act, sums3_tok)], inc=(s_x, 16))
            s_x.tick(16)
            P.dma("SP", lambda e, tc=tc: e.dma_start(
                aot[:, :], attn_own[tc:tc + 128, :]), inc=(s_x, 16))
            for k in range(4):
                s_x.tick(16)
                P.dma("SP", lambda e, k=k, tc=tc: e.dma_start(
                    gcat[:, k * 512:(k + 1) * 512], gaths[k][tc:tc + 128, :]),
                    wait=[(s_cc[k], 1), (s_dve, xa_free)], inc=(s_x, 16))
                s_x.tick(16)
                P.dma("SP", lambda e, k=k, tc=tc: e.dma_start(
                    gcat[:, FH + k * 512: FH + (k + 1) * 512],
                    gaths[k][T_OWN + tc: T_OWN + tc + 128, :]),
                    inc=(s_x, 16))
            ldx = s_x.count
            P.op("DVE", lambda e: e.tensor_add(x1full[:, 0:FH], xt3[:, 0:FH],
                                               aot[:, :]),
                 wait=[(s_x, ldx), (s_dve, x1_free), (s_sp, x1sp_free)])
            P.op("DVE", lambda e: e.scalar_tensor_tensor(
                sibscr[:, :], gcat[:, 0:FH], c_sel0[:, 0:1], xt3[:, FH:D],
                AL.mult, AL.add))
            d_x1 = s_dve.tick()
            P.op("DVE", lambda e: e.scalar_tensor_tensor(
                x1full[:, FH:D], gcat[:, FH:D], c_sel1[:, 0:1], sibscr[:, :],
                AL.mult, AL.add), inc=(s_dve, 1))
            xa_free = d_x1
            x1sp_free = s_sp.tick(16)
            x1sp_pass[p] = x1sp_free
            P.dma("PL", lambda e, tc=tc: e.dma_start(
                x1d[tc:tc + 128, :], x1full[:, :]),
                wait=[(s_dve, d_x1)], inc=(s_sp, 16))
            a2s = ln_sums(x1full, [(s_dve, d_x1), (s_dve, ln_free3[i % 2])],
                          lns2[i % 2], xt3)
            sums3_tok = a2s
            dn, st_t = ln_finish(x1full, a2s, lns2[i % 2], xnt2[i % 2])
            x1_free = dn
            ln_free3[i % 2] = st_t
            tr_tok = emit_transposes(xnt2[i % 2], dn, i * 128, xnT3)
            pe_r = s_pe.tick()
            for dc in range(32):
                P.op("PE", lambda e, dc=dc, i=i: e.matmul(
                    psR(),
                    xnT3[:, dc * TP3 + i * 128: dc * TP3 + (i + 1) * 128],
                    wrb[:, :].rearrange("p (dc f) -> p dc f", f=8)[:, dc, :],
                    start=(dc == 0), stop=(dc == 31)),
                    wait=(([(s_misc, tok_r)] + tr_tok + psr_free)
                          if dc == 0 else ()),
                    inc=(s_pe, 1) if dc == 31 else None, selfser=False)
            P.op("DVE", lambda e: e.reduce_max(r8["m1"][:, :], psR(), axis=AX),
                 wait=[(s_pe, pe_r)])
            P.op("DVE", lambda e: e.tensor_single_scalar(
                r8["nm"][:, :], r8["m1"][:, :], -1.0, AL.mult))
            P.op("DVE", lambda e: e.tensor_single_scalar(
                r8["eq"][:, :], psR(), r8["m1"][:, 0:1], AL.is_ge))
            P.op("DVE", lambda e: e.scalar_tensor_tensor(
                r8["msk"][:, :], r8["eq"][:, :], -1e30, psR(),
                AL.mult, AL.add))
            P.op("DVE", lambda e: e.reduce_max(r8["m2"][:, :], r8["msk"][:, :],
                                               axis=AX))
            d_k = s_dve.tick()
            P.op("DVE", lambda e: e.tensor_single_scalar(
                r8["ke"][:, :], psR(), r8["m2"][:, 0:1], AL.is_ge),
                inc=(s_dve, 1))
            a_ee = s_act.tick()
            P.op("ACT", lambda e: e.activation(
                r8["ee"][:, :], psR(), AF.Exp, bias=r8["nm"][:, 0:1],
                scale=1.0), wait=[(s_dve, d_k)], inc=(s_act, 1))
            d_wts = s_dve.tick()
            P.op("DVE", lambda e: e.tensor_mul(
                r8["wts"][:, :], r8["ke"][:, :], r8["ee"][:, :]),
                wait=[(s_act, a_ee)], inc=(s_dve, 1))
            psr_free = [(s_act, a_ee), (s_dve, d_wts)]
            P.op("DVE", lambda e: e.reduce_sum(r8["ssum"][:, :], r8["wts"][:, :],
                                               axis=AX))
            P.op("DVE", lambda e: e.reciprocal(r8["rc"][:, :], r8["ssum"][:, :]))
            d_w = s_dve.tick()
            P.op("DVE", lambda e: e.tensor_scalar(
                w8[:, :], r8["wts"][:, :], r8["rc"][:, 0:1], 2.0,
                AL.mult, AL.mult), inc=(s_dve, 1))
            pe_w = s_pe.tick()
            P.op("PE", lambda e: e.matmul(psW(), w8[:, :], c_idf32[:, :],
                                          start=True, stop=True),
                 wait=[(s_dve, d_w), (s_dve, wT_tok)],
                 inc=(s_pe, 1), selfser=False)
            wT_tok = s_dve.tick()
            P.op("DVE", lambda e, i=i: e.tensor_copy(
                wT_sb[:, i * 128:(i + 1) * 128], psW()),
                wait=[(s_pe, pe_w)], inc=(s_dve, 1))

        pe_we = s_pe.tick()
        P.op("PE", lambda e: e.matmul(
            psC[:, 0:TP3], c_rrep[:, :], wT_sb[:, :], start=True, stop=True),
            wait=[(s_dve, wT_tok)], inc=(s_pe, 1), selfser=False)
        pe_h = s_pe.tick()
        for dc in range(32):
            P.op("PE", lambda e, dc=dc: e.matmul(
                psA[:, 0:TP3],
                abuf[:, dc * 128:(dc + 1) * 128],
                xnT3[:, dc * TP3:(dc + 1) * TP3],
                start=(dc == 0), stop=(dc == 31)),
                wait=([(s_misc, tok_a)] if dc == 0 else ()),
                inc=(s_pe, 1) if dc == 31 else None,
                selfser=False)
        P.op("DVE", lambda e: e.tensor_copy(h_sb[:, :], psA[:, 0:TP3]),
             wait=[(s_pe, pe_h)])
        d_g = s_dve.tick()
        P.op("DVE", lambda e: e.tensor_mul(g_sb[:, :], h_sb[:, :],
                                           psC[:, 0:TP3]),
             wait=[(s_pe, pe_we)], inc=(s_dve, 1))

        NOB = D // OBW
        wt_toks = {0: wt_next}
        wt_next = None
        ev_free = {}
        for ob in range(NOB):
            buf = ob % 2
            if ob + 1 < NOB:
                wt_toks[ob + 1] = emit_wload(
                    wbT, (ob + 1) * OBW, (ob + 1) % 2, wbuf3,
                    extra_wait=[(s_pe, pe_buf3[(ob + 1) % 2])], width=OBW)
            elif p + 1 < N_PASS3:
                wt_next = emit_wload(
                    wbT, 0, (ob + 1) % 2, wbuf3,
                    extra_wait=[(s_pe, pe_buf3[(ob + 1) % 2])], width=OBW)
            for i in range(TP3 // 128):
                slot = (ob * (TP3 // 128) + i) % 2
                pst = psD if slot == 0 else psE
                pe_t = s_pe.tick()
                for dc in range(32):

                    def mo(e, dc=dc, i=i, buf=buf, pst=pst):
                        ws = wbuf3[buf][:, :].rearrange(
                            "p (dc f) -> p dc f", f=OBW)
                        return e.matmul(
                            pst[:, 0:OBW],
                            xnT3[:, dc * TP3 + i * 128:
                                 dc * TP3 + (i + 1) * 128],
                            ws[:, dc, :], start=(dc == 0), stop=False)
                    w = ()
                    if dc == 0:
                        w = ([wt_toks[ob]]
                             + ps_free["D" if slot == 0 else "E"] + tr_tok)
                    P.op("PE", mo, wait=w, selfser=False)
                P.op("PE", lambda e, ob=ob, pst=pst: e.matmul(
                    pst[:, 0:OBW], c_ones_r[:, :],
                    c_fb_bf[0:1, ob * OBW:(ob + 1) * OBW],
                    start=False, stop=False), selfser=False)
                P.op("PE", lambda e, ob=ob, i=i, pst=pst: e.matmul(
                    pst[:, 0:OBW], g_sb[:, i * 128:(i + 1) * 128],
                    bmb[:, ob * OBW:(ob + 1) * OBW],
                    start=False, stop=True),
                    wait=[(s_dve, d_g), (s_misc, tok_bm)],
                    inc=(s_pe, 1), selfser=False)
                pe_buf3[buf] = pe_t
                tok1 = s_xp[slot].tick(16)
                P.dma("SP", lambda e, i=i, ob=ob, t0=t0, slot=slot:
                      e.dma_start(
                          x1tt[slot][:, :],
                          x1d[t0 + i * 128: t0 + (i + 1) * 128,
                              ob * OBW:(ob + 1) * OBW]),
                      wait=[(s_dve, x1_busy[slot]),
                            (s_sp, x1sp_pass[p])], inc=(s_xp[slot], 16))
                d_o = s_dve.tick()
                P.op("DVE", lambda e, pst=pst, slot=slot:
                     e.tensor_add(outt[slot][:, :], pst[:, 0:OBW],
                                  x1tt[slot][:, :]),
                     wait=[(s_pe, pe_t), (s_xp[slot], tok1),
                           (s_st, st_seen[0])],
                     inc=(s_dve, 1))
                x1_busy[slot] = d_o
                st_prev = st_seen[0]
                st_seen[0] = s_st.tick(16)
                P.dma("SP", lambda e, i=i, ob=ob, t0=t0, slot=slot:
                      e.dma_start(
                          out[t0 + i * 128: t0 + (i + 1) * 128,
                              ob * OBW:(ob + 1) * OBW], outt[slot][:, :]),
                      wait=[(s_dve, d_o)], inc=(s_st, 16))
                ev_free[slot] = d_o
                ps_free["D" if slot == 0 else "E"] = [(s_dve, d_o)]

    if DEBUG:
        for r0 in range(0, T_OWN, 128):
            s_st.tick(16)
            P.dma("PL", lambda e, r0=r0: e.dma_start(
                dbg["x1"][r0:r0 + 128, :], x1d[r0:r0 + 128, :]),
                wait=[(s_sp, s_sp.count)], inc=(s_st, 16))

    P.op("SP", lambda e: e.nop(), wait=[(s_st, s_st.count)], selfser=False)

    es3.close()

    with ExitStack() as emit_es:
        for s in P.sems:
            s.h = emit_es.enter_context(nc.semaphore(s.name))
        with nc.Block() as blk:
            @blk.tensor
            def _(t):
                for fn in P.q["PE"]:
                    fn(t)

            @blk.vector
            def _(v):
                for fn in P.q["DVE"]:
                    fn(v)

            @blk.scalar
            def _(s):
                for fn in P.q["ACT"]:
                    fn(s)

            @blk.sync
            def _(s):
                for fn in P.q["SP"]:
                    fn(s)

            @blk.gpsimd
            def _(g):
                for fn in P.q["PL"]:
                    fn(g)
    es0.close()
    return nc


_NC_CACHE = None


def _get_nc():
    global _NC_CACHE
    if _NC_CACHE is None:
        _NC_CACHE = build_program()
    return _NC_CACHE


def make_in_maps(inputs):
    x = np.asarray(inputs["L0_attn_x"])
    Wq = np.asarray(inputs["L0_attn_Wq"])
    bq = np.asarray(inputs["L0_attn_bq"])
    Wk = np.asarray(inputs["L0_attn_Wk"])
    bk = np.asarray(inputs["L0_attn_bk"])
    Wv = np.asarray(inputs["L0_attn_Wv"])
    bv = np.asarray(inputs["L0_attn_bv"])
    bw = np.asarray(inputs["L0_attn_bias_weight"])
    Wb = np.asarray(inputs["L0_ffn_Wb"])
    fbias = np.asarray(inputs["L0_ffn_bias"])
    Wr = np.asarray(inputs["L0_ffn_Wr"])
    A = np.asarray(inputs["L0_ffn_A"])
    Bm = np.asarray(inputs["L0_ffn_Bm"])

    consts = {
        "biv3": _BIV3,
        "revI": _REVI,
        "diagI": _DIAGI,
        "idbf": np.eye(128, dtype=ml_dtypes.bfloat16),
        "idf32": np.eye(128, dtype=np.float32),
        "ones_r_bf": np.ones((1, 128), dtype=ml_dtypes.bfloat16),
        "ones_c_bf": np.ones((128, 1), dtype=ml_dtypes.bfloat16),
        "rrep": np.repeat(np.eye(8, dtype=np.float32), 16, axis=1),
    }
    WqT = np.ascontiguousarray(Wq.T)
    WkT = np.ascontiguousarray(Wk.T)
    WvT = np.ascontiguousarray(Wv.T)
    WbT = np.ascontiguousarray(Wb.T)
    WrT = np.ascontiguousarray(Wr.T)
    A_f = A.reshape(128, D)
    BmT_f = np.ascontiguousarray(Bm.transpose(0, 2, 1).reshape(128, D))

    in_maps = []
    perms = []
    for c in range(8):
        b, hh = c >> 1, c & 1
        pt = np.concatenate([np.arange(hh * 2048, (hh + 1) * 2048),
                             np.arange((1 - hh) * 2048, (2 - hh) * 2048)])
        pd = np.concatenate([np.arange(hh * 2048, (hh + 1) * 2048),
                             np.arange((1 - hh) * 2048, (2 - hh) * 2048)])
        perms.append(pd)
        fsl = slice(hh * 2048, (hh + 1) * 2048)
        m = dict(consts)
        m["xb"] = np.ascontiguousarray(x[b][pt][:, pd])
        m["wqT"] = np.ascontiguousarray(WqT[pd][:, fsl])
        m["wkT"] = np.ascontiguousarray(WkT[pd][:, fsl])
        m["wvT"] = np.ascontiguousarray(WvT[pd][:, fsl])
        m["wbT"] = np.ascontiguousarray(WbT[pd][:, pd])
        m["wrT"] = np.ascontiguousarray(WrT[pd])
        m["aT"] = np.ascontiguousarray(A_f.T[pd])
        m["bmT"] = np.ascontiguousarray(BmT_f[:, pd])
        m["fbias"] = np.ascontiguousarray(fbias[pd]).reshape(1, D)
        m["bq_c"] = np.ascontiguousarray(bq[fsl].reshape(16, 128).T)
        m["bk_c"] = np.ascontiguousarray(bk[fsl].reshape(16, 128).T)
        m["bv_r"] = np.ascontiguousarray(bv[fsl]).reshape(1, FH)
        m["bwT3"] = np.ascontiguousarray(bw.T[:, hh * 8:(hh + 1) * 8])
        m["sel0"] = np.full((128, 1), 1.0 if hh == 1 else 0.0, np.float32)
        m["sel1"] = np.full((128, 1), 0.0 if hh == 1 else 1.0, np.float32)
        in_maps.append(m)
    return in_maps, perms


def kernel(**inputs):
    nc = _get_nc()
    in_maps, perms = make_in_maps(inputs)
    res = run_bass_kernel_spmd(nc, in_maps, core_ids=list(range(8)))
    shards = []
    for c in range(8):
        shard = np.asarray(res.results[c]["out"])
        nat = np.empty_like(shard)
        nat[:, perms[c]] = shard
        shards.append(nat)
    full = np.concatenate(shards, axis=0)
    return full.reshape(4, 4096, D)
